# version 03
# baseline (speedup 1.0000x reference)
import numpy as np
import ml_dtypes
from contextlib import ExitStack

import concourse.bass as bass
import concourse.mybir as mybir
from concourse.bass_utils import run_bass_kernel_spmd

AF = mybir.ActivationFunctionType
ALU = mybir.AluOpType
F32 = mybir.dt.float32
BF16 = mybir.dt.bfloat16
I32 = mybir.dt.int32
AX = mybir.AxisListType.X

S_LEN = 2048
HID = 4096
HKV = 1024
IMED = 11008
NTT = 16
NTB = 4
TB = 512
FC = HID // 128
IC = IMED // 128
NH, NKV, DH = 32, 8, 128
EXP_SCALE = 1.0 / float(np.sqrt(128.0))
EXP_BIAS = -15.0
LN_EPS = 1e-5
SP_EPS = 1e-12
Q_ZERO, Q_STEP, LR = 7.5, 0.005, 0.02

ALL_PHASES = ("ln", "qkv", "gateup", "attn", "wo", "down")


def _host_consts():
    inv_freq = 1.0 / (10000.0 ** (np.arange(0, DH, 2, dtype=np.float32) / DH))
    t = np.arange(S_LEN, dtype=np.float32)
    freqs = np.outer(t, inv_freq)
    emb = np.concatenate([freqs, freqs], axis=1)
    cos_fm = np.ascontiguousarray(np.cos(emb).T.astype(np.float32))
    sin_fm = np.sin(emb).T.astype(np.float32)
    sinS_fm = np.ascontiguousarray(sin_fm * np.where(np.arange(DH)[:, None] < 64, -1.0, 1.0).astype(np.float32))
    ident = np.eye(128, dtype=ml_dtypes.bfloat16)
    prot = np.zeros((128, 128), dtype=ml_dtypes.bfloat16)
    prot[(np.arange(128) + 64) % 128, np.arange(128)] = 1
    masks = np.zeros((128, 4 * TB), dtype=ml_dtypes.bfloat16)
    jj = np.arange(128)[:, None]
    qq = np.arange(TB)[None, :]
    for r in range(4):
        masks[:, r * TB:(r + 1) * TB] = ((128 * r + jj) <= qq).astype(ml_dtypes.bfloat16)
    return ident, prot, cos_fm, sinS_fm, masks


def build_program(phases=ALL_PHASES, dbg=()):
    nc = bass.Bass("TRN2", target_bir_lowering=False)

    x_in = nc.dram_tensor("x", [S_LEN, HID], F32, kind="ExternalInput")
    wqT = nc.dram_tensor("wqT", [HID, HID], F32, kind="ExternalInput")
    wkT = nc.dram_tensor("wkT", [HID, HKV], F32, kind="ExternalInput")
    wvT = nc.dram_tensor("wvT", [HID, HKV], F32, kind="ExternalInput")
    woT = nc.dram_tensor("woT", [HID, HID], F32, kind="ExternalInput")
    qgT = nc.dram_tensor("qgT", [HID, IMED], I32, kind="ExternalInput")
    quT = nc.dram_tensor("quT", [HID, IMED], I32, kind="ExternalInput")
    qdT = nc.dram_tensor("qdT", [IMED, HID], I32, kind="ExternalInput")
    gsc_pc = nc.dram_tensor("gsc_pc", [128, IC], F32, kind="ExternalInput")
    usc_pc = nc.dram_tensor("usc_pc", [128, IC], F32, kind="ExternalInput")
    dsc_pc = nc.dram_tensor("dsc_pc", [128, FC], F32, kind="ExternalInput")
    gsc_row = nc.dram_tensor("gsc_row", [1, IMED], F32, kind="ExternalInput")
    usc_row = nc.dram_tensor("usc_row", [1, IMED], F32, kind="ExternalInput")
    dsc_row = nc.dram_tensor("dsc_row", [1, HID], F32, kind="ExternalInput")
    gA = nc.dram_tensor("gA", [HID, 32], F32, kind="ExternalInput")
    uA = nc.dram_tensor("uA", [HID, 32], F32, kind="ExternalInput")
    dA = nc.dram_tensor("dA", [IMED, 32], F32, kind="ExternalInput")
    gB = nc.dram_tensor("gB", [32, IMED], F32, kind="ExternalInput")
    uB = nc.dram_tensor("uB", [32, IMED], F32, kind="ExternalInput")
    dB = nc.dram_tensor("dB", [32, HID], F32, kind="ExternalInput")
    out = nc.dram_tensor("out", [S_LEN, HID], F32, kind="ExternalOutput")

    qT_s = nc.dram_tensor("qT_s", [HID, S_LEN], BF16)
    kT_s = nc.dram_tensor("kT_s", [HKV, S_LEN], BF16)
    vT_s = nc.dram_tensor("vT_s", [HKV, S_LEN], BF16)
    attn_s = nc.dram_tensor("attn_s", [HID, S_LEN], BF16)
    sT_s = nc.dram_tensor("sT_s", [HID, S_LEN], BF16)
    h_s = nc.dram_tensor("h_s", [IMED, S_LEN], BF16)
    dbf_s = nc.dram_tensor("dbf_s", [IMED, HID], BF16)

    ident_np, prot_np, cos_np, sinS_np, masks_np = _host_consts()
    ident_d = nc.inline_tensor(ident_np, "ident_c")
    prot_d = nc.inline_tensor(prot_np, "prot_c")
    cos_d = nc.inline_tensor(cos_np, "cos_c")
    sinS_d = nc.inline_tensor(sinS_np, "sinS_c")
    masks_d = nc.inline_tensor(masks_np, "masks_c")

    dbg_outs = {}
    for name, shape in (("bppg", [33, IMED]), ("bppu", [33, IMED]),
                        ("ugg", [33, S_LEN]), ("ugu", [33, S_LEN]),
                        ("xnT", [128, FC * S_LEN]), ("qT", [HID, S_LEN]),
                        ("kT", [HKV, S_LEN]), ("vT", [HKV, S_LEN]),
                        ("attn", [HID, S_LEN]), ("sT", [HID, S_LEN]),
                        ("h", [IMED, S_LEN]), ("dbf", [IMED, HID])):
        if name in dbg:
            dbg_outs[name] = nc.dram_tensor("dbg_" + name, shape, F32, kind="ExternalOutput")

    def slab_ap(dram, col0, ncols, nchunks):
        w = dram.shape[1]
        return bass.AP(dram, col0, [[w, 128], [128 * w, nchunks], [1, ncols]])

    def sb3(sbuf, off, d1, d2):
        row = int(np.prod(sbuf.shape[1:]))
        return bass.AP(sbuf, off, [[row, 128], list(d1), list(d2)])

    def dump_dram(scr, dram_out, sem):
        with nc.Block() as block:
            @block.gpsimd
            def _(g):
                g.dma_start(dram_out.ap(), scr.ap()).then_inc(sem.h, 16)
                g.wait_ge(sem.h, 16)

    stack_top = ExitStack()

    class Sem:
        __slots__ = ("h", "n")
        def __init__(self, name):
            self.h = stack_top.enter_context(nc.semaphore(name))
            self.n = 0
        def tag(self, instr, k=1):
            instr.then_inc(self.h, k)
            self.n += k
            return self.n

    class DSem:
        __slots__ = ("s",)
        def __init__(self, name):
            self.s = [Sem(name + "a"), Sem(name + "b")]
        def tag(self, i, instr):
            self.s[i % 2].tag(instr, 16)
        def wait(self, eng, i, per=1):
            eng.wait_ge(self.s[i % 2].h, 16 * per * (i // 2 + 1))
        def wait_all(self, eng):
            for sh in self.s:
                if sh.n:
                    eng.wait_ge(sh.h, sh.n)

    with stack_top:
        ident = stack_top.enter_context(nc.sbuf_tensor("ident", [128, 128], BF16))
        prot = stack_top.enter_context(nc.sbuf_tensor("prot", [128, 128], BF16))
        s_const = Sem("s_const")
        with nc.Block() as block:
            @block.sync
            def _(sy):
                s_const.tag(sy.dma_start(ident[:], ident_d.ap()), 16)
                s_const.tag(sy.dma_start(prot[:], prot_d.ap()), 16)
                sy.wait_ge(s_const.h, 32)

        s_pc = Sem("s_pc")
        precast_left = [86 if ("down" in phases) else 0]

        def emit_precast(g, k):
            for _ in range(k):
                if precast_left[0] == 0:
                    return
                ic = 86 - precast_left[0]
                s_pc.tag(g.dma_start(dbf_s.ap()[128 * ic:128 * (ic + 1), :],
                                     qdT.ap()[128 * ic:128 * (ic + 1), :]), 16)
                precast_left[0] -= 1

        with ExitStack() as stack_a:
            if "ln" in phases:
                xnT_res = stack_a.enter_context(nc.sbuf_tensor("xnT_res", [128, FC * S_LEN], BF16))
                _phase_ln(nc, Sem, DSem, x_in, xnT_res, ident)
                if "xnT" in dbg_outs:
                    sem = Sem("s_dbg_xnT")
                    with nc.Block() as block:
                        @block.gpsimd
                        def _(g):
                            g.dma_start(dbg_outs["xnT"].ap(), xnT_res[:]).then_inc(sem.h, 16)
                            g.wait_ge(sem.h, 16)

            if "qkv" in phases:
                _phase_qkv(nc, Sem, DSem, xnT_res, prot, cos_d, sinS_d,
                           wqT, wkT, wvT, qT_s, kT_s, vT_s, slab_ap, emit_precast)
                for nm, scr in (("qT", qT_s), ("kT", kT_s), ("vT", vT_s)):
                    if nm in dbg_outs:
                        dump_dram(scr, dbg_outs[nm], Sem("s_dbg_" + nm))

            if "gateup" in phases:
                _phase_gateup(nc, Sem, DSem, xnT_res, qgT, quT, gsc_pc, usc_pc,
                              gsc_row, usc_row, gA, uA, gB, uB, h_s, slab_ap, sb3, emit_precast,
                              dbg_sbufs={k: v for k, v in dbg_outs.items()
                                         if k in ("bppg", "bppu", "ugg", "ugu")} or None)
                if "h" in dbg_outs:
                    dump_dram(h_s, dbg_outs["h"], Sem("s_dbg_h"))

        if "attn" in phases:
            _phase_attn(nc, Sem, DSem, qT_s, kT_s, vT_s, attn_s, masks_d)
            if "attn" in dbg_outs:
                dump_dram(attn_s, dbg_outs["attn"], Sem("s_dbg_attn"))

        if "wo" in phases:
            _phase_wo(nc, Sem, DSem, woT, attn_s, sT_s, slab_ap)
            if "sT" in dbg_outs:
                dump_dram(sT_s, dbg_outs["sT"], Sem("s_dbg_sT"))

        if "dbf" in dbg_outs and "down" in phases:
            semd = Sem("s_dbg_dbf")
            with nc.Block() as block:
                @block.gpsimd
                def _(g):
                    emit_precast(g, 86)
                    g.wait_ge(s_pc.h, s_pc.n)
                    g.dma_start(dbg_outs["dbf"].ap(), dbf_s.ap()).then_inc(semd.h, 16)
                    g.wait_ge(semd.h, 16)

        if "down" in phases:
            _phase_down(nc, Sem, DSem, x_in, out, h_s, sT_s, dbf_s,
                        dsc_pc, dsc_row, dA, dB, ident, slab_ap, sb3,
                        emit_precast, s_pc)

    return nc


def _phase_ln(nc, Sem, DSem, x_in, xnT_res, ident):
    with ExitStack() as st:
        x_sb = [st.enter_context(nc.sbuf_tensor(f"x_sb{i}", [128, HID], F32)) for i in range(2)]
        xn_sb = [st.enter_context(nc.sbuf_tensor(f"xn_sb{i}", [128, HID], BF16)) for i in range(2)]
        trash = [st.enter_context(nc.sbuf_tensor(f"trash{i}", [128, HID], BF16)) for i in range(2)]
        stat = st.enter_context(nc.sbuf_tensor("stat", [128, 16], F32))
        ptr = [st.enter_context(nc.psum_tensor(f"ptr{i}", [128, 512], BF16)) for i in range(2)]
        s_x = DSem("s1_x")
        s_sq, s_var, s_rstd, s_xn, s_tr, s_evac = (
            Sem("s1_sq"), Sem("s1_var"),
            Sem("s1_rstd"), Sem("s1_xn"), Sem("s1_tr"), Sem("s1_evac"))

        def col(c):
            return stat[:, c:c + 1]

        def sb3l(sbuf, off, d1, d2):
            rw = int(np.prod(sbuf.shape[1:]))
            return bass.AP(sbuf, off, [[rw, 128], list(d1), list(d2)])

        with nc.Block() as block:
            @block.sync
            def _(sy):
                for i in range(NTT):
                    if i >= 2:
                        sy.wait_ge(s_xn.h, i - 1)
                    s_x.tag(i, sy.dma_start(x_sb[i % 2][:], x_in.ap()[128 * i:128 * (i + 1), :]))

            @block.scalar
            def _(sc):
                def rstd_pair(j):
                    b = j % 2
                    sc.wait_ge(s_var.h, j + 1)
                    if j >= 2:
                        sc.wait_ge(s_xn.h, j - 1)
                    sc.activation(col(14), col(10 + b), AF.Ln)
                    sc.activation(col(12 + b), col(14), AF.Exp, scale=-0.5)
                    s_rstd.tag(sc.copy(trash[b][:, 0:512], trash[b][:, 512:1024]))

                def evacs(jj):
                    for g in range(8):
                        G = 8 * jj + g
                        sc.wait_ge(s_tr.h, G + 1)
                        s_evac.tag(sc.copy(sb3l(xnT_res, (4 * g) * S_LEN + 128 * jj, (S_LEN, 4), (1, 128)),
                                           sb3l(ptr[G % 2], 0, (128, 4), (1, 128))))

                for i in range(NTT):
                    s_x.wait(sc, i)
                    if i >= 2:
                        sc.wait_ge(s_var.h, i - 1)
                        sc.wait_ge(s_rstd.h, i - 1)
                    s_sq.tag(sc.activation(trash[i % 2][:], x_sb[i % 2][:], AF.Square))
                    if i >= 1:
                        rstd_pair(i - 1)
                    if i >= 2:
                        evacs(i - 2)
                rstd_pair(NTT - 1)
                evacs(NTT - 2)
                evacs(NTT - 1)

            @block.vector
            def _(v):
                def stats_a(j):
                    b = j % 2
                    v.tensor_scalar(col(4 + b), col(0 + b), 1.0 / HID, None, ALU.mult)
                    v.tensor_scalar(col(6 + b), col(0 + b), col(0 + b), 1.0 / (HID * HID),
                                    ALU.mult, ALU.mult)
                    v.tensor_scalar(col(8 + b), col(2 + b), 1.0 / HID, LN_EPS, ALU.mult, ALU.add)

                def var_of(j):
                    b = j % 2
                    v.tensor_tensor(col(10 + b), col(8 + b), col(6 + b), ALU.subtract)

                def xn_of(j):
                    b = j % 2
                    v.wait_ge(s_rstd.h, j + 1)
                    if j >= 2:
                        v.wait_ge(s_tr.h, 8 * (j - 1))
                    s_xn.tag(v.tensor_scalar(xn_sb[b][:], x_sb[j % 2][:],
                                             col(4 + b), col(12 + b),
                                             ALU.subtract, ALU.mult))

                for i in range(NTT):
                    if i >= 1:
                        stats_a(i - 1)
                    s_x.wait(v, i)
                    v.reduce_sum(col(0 + i % 2), x_sb[i % 2][:], axis=AX)
                    if i >= 1:
                        var_of(i - 1)
                    v.wait_ge(s_sq.h, i + 1)
                    ins = v.reduce_sum(col(2 + i % 2), trash[i % 2][:], axis=AX)
                    if i >= 1:
                        s_var.tag(ins)
                    if i >= 1:
                        xn_of(i - 1)
                stats_a(NTT - 1)
                v.reduce_sum(col(15), x_sb[(NTT - 1) % 2][:], axis=AX)
                var_of(NTT - 1)
                s_var.tag(v.reduce_sum(col(15), x_sb[(NTT - 1) % 2][:], axis=AX))
                xn_of(NTT - 1)

            @block.tensor
            def _(t):
                for j in range(NTT):
                    t.wait_ge(s_xn.h, j + 1)
                    for g in range(8):
                        G = 8 * j + g
                        if G >= 2:
                            t.wait_ge(s_evac.h, G - 1)
                        for c4 in range(4):
                            c = 4 * g + c4
                            ins = t.transpose(ptr[G % 2][:, 128 * c4:128 * (c4 + 1)],
                                              xn_sb[j % 2][:, 128 * c:128 * (c + 1)], ident[:])
                        s_tr.tag(ins)


def _phase_qkv(nc, Sem, DSem, xnT_res, prot, cos_d, sinS_d,
               wqT, wkT, wvT, qT_s, kT_s, vT_s, slab_ap, emit_precast):
    with ExitStack() as st:
        cosf = st.enter_context(nc.sbuf_tensor("cosf", [128, S_LEN], F32))
        sinf = st.enter_context(nc.sbuf_tensor("sinf", [128, S_LEN], F32))
        wsl = [st.enter_context(nc.sbuf_tensor(f"wsl{i}", [128, FC * 128], BF16)) for i in range(2)]
        raw_bf = [st.enter_context(nc.sbuf_tensor(f"raw{i}", [128, S_LEN], BF16)) for i in range(2)]
        rope_sb = [st.enter_context(nc.sbuf_tensor(f"rope{i}", [128, S_LEN], BF16)) for i in range(2)]
        tmp1 = st.enter_context(nc.sbuf_tensor("tmp1", [128, TB], F32))
        tmp2 = st.enter_context(nc.sbuf_tensor("tmp2", [128, TB], F32))
        psp = [st.enter_context(nc.psum_tensor(f"psp{i}", [128, 4 * TB], F32)) for i in range(2)]
        s_ct = Sem("s2_ct")
        s_wld = DSem("s2_wld")
        s_proj = Sem("s2_proj")
        s_raw = Sem("s2_raw")
        s_rot = Sem("s2_rot")
        s_rope = Sem("s2_rope")
        s_store = DSem("s2_store")

        chunks = ([("q", dc) for dc in range(FC)] + [("k", dc) for dc in range(8)]
                  + [("v", dc) for dc in range(8)])
        NCH = len(chunks)
        wsrc = {"q": wqT, "k": wkT, "v": wvT}
        dsts = {"q": qT_s, "k": kT_s, "v": vT_s}

        with nc.Block() as block:
            @block.sync
            def _(sy):
                s_ct.tag(sy.dma_start(cosf[:], cos_d.ap()), 16)
                s_ct.tag(sy.dma_start(sinf[:], sinS_d.ap()), 16)
                for j, (kind, dc) in enumerate(chunks):
                    if kind == "v":
                        sy.wait_ge(s_raw.h, 4 * (j + 1))
                        src = raw_bf[j % 2]
                    else:
                        sy.wait_ge(s_rope.h, 4 * (j + 1))
                        src = rope_sb[j % 2]
                    s_store.tag(j, sy.dma_start(dsts[kind].ap()[128 * dc:128 * (dc + 1), :], src[:]))
                s_store.wait_all(sy)

            @block.gpsimd
            def _(g):
                for j, (kind, dc) in enumerate(chunks):
                    if j >= 2:
                        g.wait_ge(s_proj.h, j - 1)
                    s_wld.tag(j, g.dma_start(wsl[j % 2][:], slab_ap(wsrc[kind], 128 * dc, 128, FC)))
                    emit_precast(g, 2)

            @block.tensor
            def _(t):
                def emit_rot(i):
                    t.wait_ge(s_raw.h, 4 * (i + 1))
                    for tb in range(NTB):
                        s_rot.tag(t.matmul(psp[i % 2][:, TB * tb:TB * (tb + 1)], prot[:],
                                           raw_bf[i % 2][:, TB * tb:TB * (tb + 1)],
                                           start=True, stop=True))

                for j, (kind, dc) in enumerate(chunks):
                    s_wld.wait(t, j)
                    if j >= 2:
                        if chunks[j - 2][0] == "v":
                            t.wait_ge(s_raw.h, 4 * (j - 1))
                        else:
                            t.wait_ge(s_rope.h, 4 * (j - 1))
                    for f in range(FC):
                        for tb in range(NTB):
                            ins = t.matmul(psp[j % 2][:, TB * tb:TB * (tb + 1)],
                                           wsl[j % 2][:, 128 * f:128 * (f + 1)],
                                           xnT_res[:, S_LEN * f + TB * tb:S_LEN * f + TB * (tb + 1)],
                                           start=(f == 0), stop=(f == FC - 1))
                    s_proj.tag(ins)
                    if j >= 1 and chunks[j - 1][0] != "v":
                        emit_rot(j - 1)
                if chunks[NCH - 1][0] != "v":
                    emit_rot(NCH - 1)

            @block.scalar
            def _(sc):
                for j, (kind, dc) in enumerate(chunks):
                    sc.wait_ge(s_proj.h, j + 1)
                    if j >= 2:
                        s_store.wait(sc, j - 2)
                        if chunks[j - 2][0] != "v":
                            sc.wait_ge(s_rope.h, 4 * (j - 1))
                    for tb in range(NTB):
                        s_raw.tag(sc.copy(raw_bf[j % 2][:, TB * tb:TB * (tb + 1)],
                                          psp[j % 2][:, TB * tb:TB * (tb + 1)]))

            @block.vector
            def _(v):
                v.wait_ge(s_ct.h, 32)
                for j, (kind, dc) in enumerate(chunks):
                    if kind == "v":
                        continue
                    for tb in range(NTB):
                        v.wait_ge(s_raw.h, 4 * j + tb + 1)
                        if tb == 0 and j >= 2:
                            s_store.wait(v, j - 2)
                        v.tensor_tensor(tmp1[:], raw_bf[j % 2][:, TB * tb:TB * (tb + 1)],
                                        cosf[:, TB * tb:TB * (tb + 1)], ALU.mult)
                        v.wait_ge(s_rot.h, 4 * j + tb + 1)
                        v.tensor_tensor(tmp2[:], psp[j % 2][:, TB * tb:TB * (tb + 1)],
                                        sinf[:, TB * tb:TB * (tb + 1)], ALU.mult)
                        s_rope.tag(v.tensor_tensor(rope_sb[j % 2][:, TB * tb:TB * (tb + 1)],
                                                   tmp1[:], tmp2[:], ALU.add))


def _phase_gateup(nc, Sem, DSem, xnT_res, qgT, quT, gsc_pc, usc_pc,
                  gsc_row, usc_row, gA, uA, gB, uB, h_s, slab_ap, sb3, emit_precast,
                  dbg_sbufs=None):
    with ExitStack() as st:
        bpp = [st.enter_context(nc.sbuf_tensor(f"bpp{i}", [33, IMED], BF16)) for i in range(2)]
        ug_sb = [st.enter_context(nc.sbuf_tensor(f"ug{i}", [33, S_LEN], BF16)) for i in range(2)]
        spc = [st.enter_context(nc.sbuf_tensor(f"spc{i}", [128, IC], F32)) for i in range(2)]
        sg_sb = [st.enter_context(nc.sbuf_tensor(f"sg{i}", [128, TB], BF16)) for i in range(4)]
        uu_sb = [st.enter_context(nc.sbuf_tensor(f"uu{i}", [128, TB], BF16)) for i in range(4)]
        h_sb = [st.enter_context(nc.sbuf_tensor(f"hsb{i}", [128, TB], BF16)) for i in range(2)]

        s_pl = Sem("s5_pl")
        s_pg = Sem("s5_pg")
        s_umm = Sem("s5_umm")
        s_uev = Sem("s5_uev")
        s_bld = DSem("s5_bld")
        s_rsc = Sem("s5_rsc")
        s_bmm = Sem("s5_bmm")
        s_btt = Sem("s5_btt")
        s_wg = Sem("s5_wg")
        s_wu = Sem("s5_wu")
        s_gp = Sem("s5_gp")
        s_up = Sem("s5_up")
        s_sil = Sem("s5_sil")
        s_cpy = Sem("s5_cpy")
        s_h = Sem("s5_h")
        s_hd = DSem("s5_hd")

        with ExitStack() as pre:
            aaug = [pre.enter_context(nc.sbuf_tensor(f"aaug{i}", [128, FC * 33], BF16)) for i in range(2)]
            sclf = [pre.enter_context(nc.sbuf_tensor(f"sclf{i}", [1, TB], F32)) for i in range(2)]
            sclm = pre.enter_context(nc.sbuf_tensor("sclm", [1, TB], F32))
            rscb = [pre.enter_context(nc.sbuf_tensor(f"rscb{i}", [1, TB], BF16)) for i in range(2)]
            bst = [pre.enter_context(nc.sbuf_tensor(f"bst{i}", [32, TB], F32)) for i in range(2)]
            one1x32 = pre.enter_context(nc.sbuf_tensor("one1x32", [1, 32], BF16))
            ps_a = pre.enter_context(nc.psum_tensor("ps_a", [33, TB], F32))
            ps_bc = pre.enter_context(nc.psum_tensor("ps_bc", [32, TB], F32))

            with nc.Block() as block:
                @block.sync
                def _(sy):
                    s_pl.tag(sy.dma_start(spc[0][:], gsc_pc.ap()), 16)
                    s_pl.tag(sy.dma_start(spc[1][:], usc_pc.ap()), 16)

                @block.gpsimd
                def _(g):
                    s_pg.tag(g.dma_start(sb3(aaug[0], 0, (33, FC), (1, 32)),
                                         bass.AP(gA, 0, [[32, 128], [128 * 32, FC], [1, 32]])), 16)
                    s_pg.tag(g.dma_start(sb3(aaug[1], 0, (33, FC), (1, 32)),
                                         bass.AP(uA, 0, [[32, 128], [128 * 32, FC], [1, 32]])), 16)

                @block.vector
                def _(v):
                    v.memset(one1x32[:], 1.0)
                    v.wait_ge(s_pg.h, 32)
                    for gg in range(2):
                        v.memset(bass.AP(aaug[gg], 32, [[FC * 33, 128], [33, FC]]), 1.0)
                    v.wait_ge(s_pl.h, 32)
                    for gg in range(2):
                        v.tensor_scalar(spc[gg][:], spc[gg][:], Q_STEP, SP_EPS, ALU.mult, ALU.max)

            with nc.Block() as block:
                @block.tensor
                def _(t):
                    for gg in range(2):
                        for tb in range(NTB):
                            gi = 4 * gg + tb
                            if gi >= 1:
                                t.wait_ge(s_uev.h, gi)
                            for f in range(FC):
                                ins = t.matmul(ps_a[:], aaug[gg][:, 33 * f:33 * (f + 1)],
                                               xnT_res[:, S_LEN * f + TB * tb:S_LEN * f + TB * (tb + 1)],
                                               start=(f == 0), stop=(f == FC - 1))
                            s_umm.tag(ins)

                @block.scalar
                def _(sc):
                    for gi in range(8):
                        sc.wait_ge(s_umm.h, gi + 1)
                        gg, tb = gi // 4, gi % 4
                        s_uev.tag(sc.copy(ug_sb[gg][:, TB * tb:TB * (tb + 1)], ps_a[:]))

            NCHB = (IMED + TB - 1) // TB
            chw = [min(TB, IMED - TB * c) for c in range(NCHB)]
            with nc.Block() as block:
                @block.sync
                def _(sy):
                    for gg, scrow in ((0, gsc_row), (1, usc_row)):
                        Bsrc = gB if gg == 0 else uB
                        for ch in range(NCHB):
                            gi = gg * NCHB + ch
                            w = chw[ch]
                            if gi >= 2:
                                sy.wait_ge(s_btt.h, gi - 1)
                            s_bld.tag(gi, sy.dma_start(sclf[gi % 2][:, 0:w], scrow.ap()[:, TB * ch:TB * ch + w]))
                            s_bld.tag(gi, sy.dma_start(bst[gi % 2][:, 0:w], Bsrc.ap()[:, TB * ch:TB * ch + w]))

                @block.vector
                def _(v):
                    for gg in range(2):
                        for ch in range(NCHB):
                            gi = gg * NCHB + ch
                            w = chw[ch]
                            s_bld.wait(v, gi, per=2)
                            v.tensor_scalar(sclm[:, 0:w], sclf[gi % 2][:, 0:w], Q_STEP, SP_EPS, ALU.mult, ALU.max)
                            v.reciprocal(sclm[:, 0:w], sclm[:, 0:w])
                            s_rsc.tag(v.tensor_scalar(rscb[gi % 2][:, 0:w], sclm[:, 0:w], LR, None, ALU.mult))
                            v.wait_ge(s_bmm.h, gi + 1)
                            s_btt.tag(v.tensor_tensor(bpp[gg][0:32, TB * ch:TB * ch + w],
                                                      bst[gi % 2][:, 0:w], ps_bc[:, 0:w], ALU.mult))
                        v.memset(bpp[gg][32:33, :], -Q_ZERO)

                @block.tensor
                def _(t):
                    for gg in range(2):
                        for ch in range(NCHB):
                            gi = gg * NCHB + ch
                            w = chw[ch]
                            t.wait_ge(s_rsc.h, gi + 1)
                            if gi >= 1:
                                t.wait_ge(s_btt.h, gi)
                            s_bmm.tag(t.matmul(ps_bc[:, 0:w], one1x32[:], rscb[gi % 2][:, 0:w],
                                               start=True, stop=True))

        if dbg_sbufs is not None:
            semb = Sem("s5_dbgb")
            with nc.Block() as block:
                @block.gpsimd
                def _(g):
                    k = 0
                    for nm, sbuf in (("bppg", bpp[0]), ("bppu", bpp[1]), ("ugg", ug_sb[0]), ("ugu", ug_sb[1])):
                        if nm in dbg_sbufs:
                            k += 16
                            g.dma_start(dbg_sbufs[nm].ap(), sbuf[:]).then_inc(semb.h, 16)
                    if k:
                        g.wait_ge(semb.h, k)

        gsl = st.enter_context(nc.sbuf_tensor("gsl", [128, FC * 128], BF16))
        usl = st.enter_context(nc.sbuf_tensor("usl", [128, FC * 128], BF16))
        ps_g = [stackless_psum(nc, st, f"psg{i}", [128, 2 * TB]) for i in range(2)]
        ps_u = [stackless_psum(nc, st, f"psu{i}", [128, 2 * TB]) for i in range(2)]

        def psbank(ps, tb):
            return ps[tb // 2][:, TB * (tb % 2):TB * (tb % 2 + 1)]

        with nc.Block() as block:
            @block.gpsimd
            def _(g):
                for oc in range(IC):
                    if oc >= 1:
                        g.wait_ge(s_gp.h, oc)
                    s_wg.tag(g.dma_start(gsl[:], slab_ap(qgT, 128 * oc, 128, FC)), 16)
                    emit_precast(g, 1)
                    if oc >= 1:
                        g.wait_ge(s_up.h, oc)
                    s_wu.tag(g.dma_start(usl[:], slab_ap(quT, 128 * oc, 128, FC)), 16)

            @block.tensor
            def _(t):
                for oc in range(IC):
                    t.wait_ge(s_wg.h, 16 * (oc + 1))
                    if oc >= 1:
                        t.wait_ge(s_sil.h, 4 * oc)
                    for f in range(FC):
                        for tb in range(NTB):
                            t.matmul(psbank(ps_g, tb),
                                     gsl[:, 128 * f:128 * (f + 1)],
                                     xnT_res[:, S_LEN * f + TB * tb:S_LEN * f + TB * (tb + 1)],
                                     start=(f == 0), stop=False)
                    for tb in range(NTB):
                        last = t.matmul(psbank(ps_g, tb),
                                        bpp[0][:, 128 * oc:128 * (oc + 1)],
                                        ug_sb[0][:, TB * tb:TB * (tb + 1)],
                                        start=False, stop=True)
                    s_gp.tag(last)
                    t.wait_ge(s_wu.h, 16 * (oc + 1))
                    if oc >= 1:
                        t.wait_ge(s_cpy.h, 4 * oc)
                    for f in range(FC):
                        for tb in range(NTB):
                            t.matmul(psbank(ps_u, tb),
                                     usl[:, 128 * f:128 * (f + 1)],
                                     xnT_res[:, S_LEN * f + TB * tb:S_LEN * f + TB * (tb + 1)],
                                     start=(f == 0), stop=False)
                    for tb in range(NTB):
                        last = t.matmul(psbank(ps_u, tb),
                                        bpp[1][:, 128 * oc:128 * (oc + 1)],
                                        ug_sb[1][:, TB * tb:TB * (tb + 1)],
                                        start=False, stop=True)
                    s_up.tag(last)

            @block.scalar
            def _(sc):
                cnt = 0
                for oc in range(IC):
                    sc.wait_ge(s_gp.h, oc + 1)
                    for tb in range(NTB):
                        cnt += 1
                        if cnt > 4:
                            sc.wait_ge(s_h.h, cnt - 4)
                        s_sil.tag(sc.activation(sg_sb[cnt % 4][:], psbank(ps_g, tb),
                                                AF.Silu, scale=spc[0][:, oc:oc + 1]))
                    sc.wait_ge(s_up.h, oc + 1)
                    for tb in range(NTB):
                        cnt2 = 4 * oc + tb + 1
                        if cnt2 > 4:
                            sc.wait_ge(s_h.h, cnt2 - 4)
                        s_cpy.tag(sc.activation(uu_sb[cnt2 % 4][:], psbank(ps_u, tb),
                                                AF.Copy, scale=spc[1][:, oc:oc + 1]))

            @block.vector
            def _(v):
                cnt = 0
                for oc in range(IC):
                    for tb in range(NTB):
                        cnt += 1
                        v.wait_ge(s_cpy.h, cnt)
                        if cnt > 2:
                            s_hd.wait(v, cnt - 3)
                        s_h.tag(v.tensor_tensor(h_sb[cnt % 2][:], sg_sb[cnt % 4][:], uu_sb[cnt % 4][:], ALU.mult))

            @block.sync
            def _(sy):
                cnt = 0
                for oc in range(IC):
                    for tb in range(NTB):
                        cnt += 1
                        sy.wait_ge(s_h.h, cnt)
                        s_hd.tag(cnt - 1, sy.dma_start(h_s.ap()[128 * oc:128 * (oc + 1), TB * tb:TB * (tb + 1)],
                                                       h_sb[cnt % 2][:]))
                s_hd.wait_all(sy)


def stackless_psum(nc, st, name, shape):
    return st.enter_context(nc.psum_tensor(name, shape, F32))


def _phase_attn(nc, Sem, DSem, qT_s, kT_s, vT_s, attn_s, masks_d):
    with ExitStack() as st:
        kres = st.enter_context(nc.sbuf_tensor("kres", [128, NKV * S_LEN], BF16))
        vres = st.enter_context(nc.sbuf_tensor("vres", [128, NTT * HKV], BF16))
        maskr = st.enter_context(nc.sbuf_tensor("maskr", [128, 4 * TB], BF16))
        qt_sb = [st.enter_context(nc.sbuf_tensor(f"qt{i}", [128, TB], BF16)) for i in range(2)]
        et_sb = [st.enter_context(nc.sbuf_tensor(f"et{i}", [128, TB], BF16)) for i in range(4)]
        etm_sb = [st.enter_context(nc.sbuf_tensor(f"etm{i}", [128, TB], BF16)) for i in range(4)]
        rz_f = [st.enter_context(nc.sbuf_tensor(f"rzf{i}", [1, TB], F32)) for i in range(2)]
        rz_bf = [st.enter_context(nc.sbuf_tensor(f"rzb{i}", [1, TB], BF16)) for i in range(2)]
        rzb_sb = [st.enter_context(nc.sbuf_tensor(f"rzbs{i}", [128, TB], BF16)) for i in range(2)]
        ao_sb = [st.enter_context(nc.sbuf_tensor(f"ao{i}", [128, TB], BF16)) for i in range(2)]
        ones_z = st.enter_context(nc.sbuf_tensor("ones_z", [128, 1], BF16))
        ones_r = st.enter_context(nc.sbuf_tensor("ones_r", [1, 128], BF16))
        ebias = st.enter_context(nc.sbuf_tensor("ebias", [128, 1], F32))
        ps_s = [st.enter_context(nc.psum_tensor(f"pss{i}", [128, TB], F32)) for i in range(4)]
        ps_av = [st.enter_context(nc.psum_tensor(f"psav{i}", [128, TB], F32)) for i in range(2)]
        ps_z = [st.enter_context(nc.psum_tensor(f"psz{i}", [128, TB], F32)) for i in range(2)]

        s_res = Sem("s3_res")
        s_qld = DSem("s3_qld")
        s_s = Sem("s3_s")
        s_e = Sem("s3_e")
        s_m = Sem("s3_m")
        s_av = Sem("s3_av")
        s_rz = Sem("s3_rz")
        s_b = Sem("s3_b")
        s_rzb = Sem("s3_rzb")
        s_attn = Sem("s3_attn")
        s_ast = DSem("s3_ast")

        heads = [(Q, h) for Q in range(NTB) for h in range(NH)]
        NHD = len(heads)
        chunk_no = {}
        exp_waits = {}
        msk_cum = [0]
        head_last = []
        c = 0
        m = 0
        for idx, (Q, h) in enumerate(heads):
            for j in range(4 * (Q + 1)):
                c += 1
                chunk_no[(idx, j)] = c
                if j >= 4 * Q:
                    m += 1
                    exp_waits[(idx, j)] = ("m", m)
                else:
                    exp_waits[(idx, j)] = ("e", c)
                msk_cum.append(m)
            head_last.append(c)
        NC_TOT = c

        with nc.Block() as block:
            @block.vector
            def _(v):
                v.memset(ones_z[:], 1.0)
                v.memset(ones_r[:], 1.0)
                v.memset(ebias[:], EXP_BIAS)

        with nc.Block() as block:
            @block.sync
            def _(sy):
                for g in range(NKV):
                    s_res.tag(sy.dma_start(kres[:, S_LEN * g:S_LEN * (g + 1)],
                                           kT_s.ap()[128 * g:128 * (g + 1), :]), 16)
                s_res.tag(sy.dma_start(maskr[:], masks_d.ap()), 16)
                for jc in range(NTT):
                    for g in range(NKV):
                        s_res.tag(sy.dma_start_transpose(
                            vres[:, HKV * jc + 128 * g:HKV * jc + 128 * (g + 1)],
                            vT_s.ap()[128 * g:128 * (g + 1), 128 * jc:128 * (jc + 1)]), 16)
                for idx in range(NHD):
                    Q, h = heads[idx]
                    if idx >= 2:
                        sy.wait_ge(s_s.h, head_last[idx - 2])
                    s_qld.tag(idx, sy.dma_start(qt_sb[idx % 2][:],
                                                qT_s.ap()[128 * h:128 * (h + 1), TB * Q:TB * (Q + 1)]))
                    if idx >= 1:
                        Qp, hp = heads[idx - 1]
                        sy.wait_ge(s_attn.h, idx)
                        s_ast.tag(idx - 1, sy.dma_start(attn_s.ap()[128 * hp:128 * (hp + 1), TB * Qp:TB * (Qp + 1)],
                                                        ao_sb[(idx - 1) % 2][:]))
                Qp, hp = heads[NHD - 1]
                sy.wait_ge(s_attn.h, NHD)
                s_ast.tag(NHD - 1, sy.dma_start(attn_s.ap()[128 * hp:128 * (hp + 1), TB * Qp:TB * (Qp + 1)],
                                                ao_sb[(NHD - 1) % 2][:]))
                s_ast.wait_all(sy)

            @block.vector
            def _(v):
                for idx in range(NHD):
                    Q, h = heads[idx]
                    for j in range(4 * Q, 4 * (Q + 1)):
                        cc = chunk_no[(idx, j)]
                        v.wait_ge(s_e.h, cc)
                        if cc > 4:
                            v.wait_ge(s_av.h, cc - 4)
                        r = j - 4 * Q
                        s_m.tag(v.tensor_tensor(etm_sb[cc % 4][:], et_sb[cc % 4][:],
                                                maskr[:, TB * r:TB * (r + 1)], ALU.mult))
                    v.wait_ge(s_av.h, head_last[idx])
                    v.reciprocal(rz_f[idx % 2][:], ps_z[idx % 2][0:1, :])
                    s_rz.tag(v.tensor_copy(rz_bf[idx % 2][:], rz_f[idx % 2][:]))
                    v.wait_ge(s_rzb.h, idx + 1)
                    if idx >= 2:
                        s_ast.wait(v, idx - 2)
                    s_attn.tag(v.tensor_tensor(ao_sb[idx % 2][:], ps_av[idx % 2][:],
                                               rzb_sb[idx % 2][:], ALU.mult))

            @block.scalar
            def _(sc):
                for idx in range(NHD):
                    Q, h = heads[idx]
                    for j in range(4 * (Q + 1)):
                        cc = chunk_no[(idx, j)]
                        sc.wait_ge(s_s.h, cc)
                        if cc > 4:
                            sc.wait_ge(s_av.h, cc - 4)
                            sc.wait_ge(s_m.h, msk_cum[cc - 4])
                        s_e.tag(sc.activation(et_sb[cc % 4][:], ps_s[(cc - 1) % 4][:], AF.Exp,
                                              scale=EXP_SCALE, bias=ebias[:, 0:1]))
                    sc.wait_ge(s_b.h, idx + 1)
                    s_rzb.tag(sc.copy(rzb_sb[idx % 2][:], ps_z[idx % 2][:]))

            @block.tensor
            def _(t):
                t.wait_ge(s_res.h, s_res.n)

                def emit_avz(idx, j, n_j):
                    Q, h = heads[idx]
                    g = h // 4
                    cc = chunk_no[(idx, j)]
                    kind, val = exp_waits[(idx, j)]
                    t.wait_ge(s_m.h if kind == "m" else s_e.h, val)
                    esrc = etm_sb[cc % 4] if kind == "m" else et_sb[cc % 4]
                    t.matmul(ps_av[idx % 2][:],
                             vres[:, HKV * j + 128 * g:HKV * j + 128 * (g + 1)],
                             esrc[:], start=(j == 0), stop=(j == n_j - 1))
                    s_av.tag(t.matmul(ps_z[idx % 2][0:1, :], ones_z[:], esrc[:],
                                      start=(j == 0), stop=(j == n_j - 1)))

                for idx in range(NHD):
                    Q, h = heads[idx]
                    g = h // 4
                    n_j = 4 * (Q + 1)
                    s_qld.wait(t, idx)
                    if idx >= 2:
                        t.wait_ge(s_attn.h, idx - 1)
                        t.wait_ge(s_rzb.h, idx - 1)
                    for j in range(n_j):
                        cc = chunk_no[(idx, j)]
                        if cc > 4:
                            t.wait_ge(s_e.h, cc - 4)
                        s_s.tag(t.matmul(ps_s[(cc - 1) % 4][:],
                                         kres[:, S_LEN * g + 128 * j:S_LEN * g + 128 * (j + 1)],
                                         qt_sb[idx % 2][:], start=True, stop=True))
                        if j >= 3:
                            emit_avz(idx, j - 3, n_j)
                    for jt in range(max(0, n_j - 3), n_j):
                        emit_avz(idx, jt, n_j)
                    t.wait_ge(s_rz.h, idx + 1)
                    s_b.tag(t.matmul(ps_z[idx % 2][:], ones_r[:], rz_bf[idx % 2][:],
                                     start=True, stop=True))


def _phase_wo(nc, Sem, DSem, woT, attn_s, sT_s, slab_ap):
    with ExitStack() as st:
        ares = st.enter_context(nc.sbuf_tensor("ares", [128, FC * S_LEN], BF16))
        wsl4 = [st.enter_context(nc.sbuf_tensor(f"wsl4{i}", [128, FC * 128], BF16)) for i in range(2)]
        s_sb4 = [st.enter_context(nc.sbuf_tensor(f"ssb4{i}", [128, S_LEN], BF16)) for i in range(2)]
        ps4 = [st.enter_context(nc.psum_tensor(f"ps4{i}", [128, 4 * TB], F32)) for i in range(2)]
        s_ares = Sem("s4_ares")
        s_wld4 = DSem("s4_wld")
        s_p4 = Sem("s4_p")
        s_ev4 = Sem("s4_ev")
        s_st4 = DSem("s4_st")

        with nc.Block() as block:
            @block.sync
            def _(sy):
                for f in range(FC):
                    s_ares.tag(sy.dma_start(ares[:, S_LEN * f:S_LEN * (f + 1)],
                                            attn_s.ap()[128 * f:128 * (f + 1), :]), 16)
                for oc in range(FC):
                    sy.wait_ge(s_ev4.h, 4 * (oc + 1))
                    s_st4.tag(oc, sy.dma_start(sT_s.ap()[128 * oc:128 * (oc + 1), :], s_sb4[oc % 2][:]))
                s_st4.wait_all(sy)

            @block.gpsimd
            def _(g):
                for oc in range(FC):
                    if oc >= 2:
                        g.wait_ge(s_p4.h, oc - 1)
                    s_wld4.tag(oc, g.dma_start(wsl4[oc % 2][:], slab_ap(woT, 128 * oc, 128, FC)))

            @block.tensor
            def _(t):
                for oc in range(FC):
                    s_wld4.wait(t, oc)
                    if oc == 0:
                        t.wait_ge(s_ares.h, 16 * FC)
                    if oc >= 2:
                        t.wait_ge(s_ev4.h, 4 * (oc - 1))
                    for f in range(FC):
                        for tb in range(NTB):
                            ins = t.matmul(ps4[oc % 2][:, TB * tb:TB * (tb + 1)],
                                           wsl4[oc % 2][:, 128 * f:128 * (f + 1)],
                                           ares[:, S_LEN * f + TB * tb:S_LEN * f + TB * (tb + 1)],
                                           start=(f == 0), stop=(f == FC - 1))
                    s_p4.tag(ins)

            @block.scalar
            def _(sc):
                for oc in range(FC):
                    sc.wait_ge(s_p4.h, oc + 1)
                    if oc >= 2:
                        s_st4.wait(sc, oc - 2)
                    for tb in range(NTB):
                        s_ev4.tag(sc.copy(s_sb4[oc % 2][:, TB * tb:TB * (tb + 1)],
                                          ps4[oc % 2][:, TB * tb:TB * (tb + 1)]))


def _phase_down(nc, Sem, DSem, x_in, out, h_s, sT_s, dbf_s,
                dsc_pc, dsc_row, dA, dB, ident, slab_ap, sb3, emit_precast, s_pc):
    with ExitStack() as st:
        hres = st.enter_context(nc.sbuf_tensor("hres", [128, IC * TB], BF16))
        dsb = [st.enter_context(nc.sbuf_tensor(f"dsb{i}", [128, IC * 128], BF16)) for i in range(2)]
        aaugd = st.enter_context(nc.sbuf_tensor("aaugd", [128, IC * 33], BF16))
        bppd = st.enter_context(nc.sbuf_tensor("bppd", [33, HID], BF16))
        ud_sb = st.enter_context(nc.sbuf_tensor("ud_sb", [33, TB], BF16))
        spcd = st.enter_context(nc.sbuf_tensor("spcd", [128, FC], F32))
        g_sb = [st.enter_context(nc.sbuf_tensor(f"g6{i}", [128, TB], BF16)) for i in range(2)]
        m_sb = [st.enter_context(nc.sbuf_tensor(f"m6{i}", [128, TB], BF16)) for i in range(2)]
        st_sb = [st.enter_context(nc.sbuf_tensor(f"st6{i}", [128, TB], BF16)) for i in range(2)]
        x_sb6 = [st.enter_context(nc.sbuf_tensor(f"x6{i}", [128, TB], F32)) for i in range(2)]
        o_sb = [st.enter_context(nc.sbuf_tensor(f"o6{i}", [128, TB], F32)) for i in range(2)]

        s_pl6 = Sem("s6_pl")
        s_bl6 = DSem("s6_bl")
        s_pg6 = Sem("s6_pg")
        s_rsc6 = Sem("s6_rsc")
        s_bmm6 = Sem("s6_bmm")
        s_btt6 = Sem("s6_btt")
        s_hld = DSem("s6_hld")
        s_dsl = DSem("s6_dsl")
        s_ud = Sem("s6_ud")
        s_udev = Sem("s6_udev")
        s_d = Sem("s6_d")
        s_g = Sem("s6_g")
        s_sld = DSem("s6_sld")
        s_xld = DSem("s6_xld")
        s_mm6 = Sem("s6_m")
        s_t6 = Sem("s6_t")
        s_o6 = Sem("s6_o")
        s_ost = DSem("s6_ost")

        with ExitStack() as pre:
            sclf = [pre.enter_context(nc.sbuf_tensor(f"sclf6{i}", [1, TB], F32)) for i in range(2)]
            sclm = pre.enter_context(nc.sbuf_tensor("sclm6", [1, TB], F32))
            rscb = [pre.enter_context(nc.sbuf_tensor(f"rscb6{i}", [1, TB], BF16)) for i in range(2)]
            bstd = [pre.enter_context(nc.sbuf_tensor(f"bstd{i}", [32, TB], F32)) for i in range(2)]
            one1x32d = pre.enter_context(nc.sbuf_tensor("one1x32d", [1, 32], BF16))
            ps_bcd = pre.enter_context(nc.psum_tensor("ps_bcd", [32, TB], F32))
            NCHB = HID // TB

            with nc.Block() as block:
                @block.sync
                def _(sy):
                    s_pl6.tag(sy.dma_start(spcd[:], dsc_pc.ap()), 16)
                    for ch in range(NCHB):
                        if ch >= 2:
                            sy.wait_ge(s_btt6.h, ch - 1)
                        s_bl6.tag(ch, sy.dma_start(sclf[ch % 2][:], dsc_row.ap()[:, TB * ch:TB * (ch + 1)]))
                        s_bl6.tag(ch, sy.dma_start(bstd[ch % 2][:], dB.ap()[:, TB * ch:TB * (ch + 1)]))

                @block.gpsimd
                def _(g):
                    emit_precast(g, 86)
                    s_pg6.tag(g.dma_start(sb3(aaugd, 0, (33, IC), (1, 32)),
                                          bass.AP(dA, 0, [[32, 128], [128 * 32, IC], [1, 32]])), 16)

                @block.vector
                def _(v):
                    v.memset(one1x32d[:], 1.0)
                    v.wait_ge(s_pg6.h, 16)
                    v.memset(bass.AP(aaugd, 32, [[IC * 33, 128], [33, IC]]), 1.0)
                    v.wait_ge(s_pl6.h, 16)
                    v.tensor_scalar(spcd[:], spcd[:], Q_STEP, SP_EPS, ALU.mult, ALU.max)
                    for ch in range(NCHB):
                        s_bl6.wait(v, ch, per=2)
                        v.tensor_scalar(sclm[:], sclf[ch % 2][:], Q_STEP, SP_EPS, ALU.mult, ALU.max)
                        v.reciprocal(sclm[:], sclm[:])
                        s_rsc6.tag(v.tensor_scalar(rscb[ch % 2][:], sclm[:], LR, None, ALU.mult))
                        v.wait_ge(s_bmm6.h, ch + 1)
                        s_btt6.tag(v.tensor_tensor(bppd[0:32, TB * ch:TB * (ch + 1)],
                                                   bstd[ch % 2][:], ps_bcd[:], ALU.mult))
                    v.memset(bppd[32:33, :], -Q_ZERO)

                @block.tensor
                def _(t):
                    for ch in range(NCHB):
                        t.wait_ge(s_rsc6.h, ch + 1)
                        if ch >= 1:
                            t.wait_ge(s_btt6.h, ch)
                        s_bmm6.tag(t.matmul(ps_bcd[:], one1x32d[:], rscb[ch % 2][:],
                                            start=True, stop=True))

        ps_d = [st.enter_context(nc.psum_tensor(f"psd{i}", [128, TB], F32)) for i in range(2)]
        ps_t = [st.enter_context(nc.psum_tensor(f"pst{i}", [128, TB], BF16)) for i in range(2)]
        ps_ud = st.enter_context(nc.psum_tensor("psud", [33, TB], F32))

        NTOT = NTB * FC
        with nc.Block() as block:
            @block.gpsimd
            def _(g):
                g.wait_ge(s_pc.h, 16 * 86)
                n = 0
                for tc in range(NTB):
                    for oc in range(FC):
                        n += 1
                        if n > 2:
                            g.wait_ge(s_d.h, n - 2)
                        s_dsl.tag(n - 1, g.dma_start(dsb[n % 2][:], slab_ap(dbf_s, 128 * oc, 128, IC)))

            @block.sync
            def _(sy):
                n = 0
                for tc in range(NTB):
                    if tc >= 1:
                        sy.wait_ge(s_d.h, FC * tc)
                    for piece in range(8):
                        ic0 = piece * 11
                        icn = min(11, IC - ic0)
                        if icn <= 0:
                            continue
                        ap_in = bass.AP(h_s, 128 * ic0 * S_LEN + TB * tc,
                                        [[S_LEN, 128], [128 * S_LEN, icn], [1, TB]])
                        s_hld.tag(tc, sy.dma_start(sb3(hres, TB * ic0, (TB, icn), (1, TB)), ap_in))
                    for oc in range(FC):
                        n += 1
                        if n > 2:
                            sy.wait_ge(s_mm6.h, n - 2)
                        s_sld.tag(n - 1, sy.dma_start(st_sb[n % 2][:],
                                                      sT_s.ap()[128 * oc:128 * (oc + 1), TB * tc:TB * (tc + 1)]))
                        if n > 2:
                            sy.wait_ge(s_o6.h, n - 2)
                        s_xld.tag(n - 1, sy.dma_start(x_sb6[n % 2][:],
                                                      bass.AP(x_in, TB * tc * HID + 128 * oc,
                                                              [[HID, 128], [128 * HID, 4], [1, 128]])))
                        if n >= 2:
                            sy.wait_ge(s_o6.h, n - 1)
                            tcp, ocp = divmod(n - 2, FC)
                            s_ost.tag(n - 2, sy.dma_start(bass.AP(out, TB * tcp * HID + 128 * ocp,
                                                                  [[HID, 128], [128 * HID, 4], [1, 128]]),
                                                          o_sb[(n - 1) % 2][:]))
                sy.wait_ge(s_o6.h, NTOT)
                tcp, ocp = divmod(NTOT - 1, FC)
                s_ost.tag(NTOT - 1, sy.dma_start(bass.AP(out, TB * tcp * HID + 128 * ocp,
                                                         [[HID, 128], [128 * HID, 4], [1, 128]]),
                                                 o_sb[NTOT % 2][:]))
                s_ost.wait_all(sy)

            @block.tensor
            def _(t):
                n = 0
                for tc in range(NTB):
                    s_hld.wait(t, tc, per=8)
                    if tc >= 1:
                        t.wait_ge(s_udev.h, tc)
                    for ic in range(IC):
                        ins = t.matmul(ps_ud[:], aaugd[:, 33 * ic:33 * (ic + 1)],
                                       hres[:, TB * ic:TB * (ic + 1)],
                                       start=(ic == 0), stop=(ic == IC - 1))
                    s_ud.tag(ins)
                    for oc in range(FC):
                        n += 1
                        s_dsl.wait(t, n - 1)
                        if n > 2:
                            t.wait_ge(s_g.h, n - 2)
                        if oc == 0:
                            t.wait_ge(s_udev.h, tc + 1)
                        for ic in range(IC):
                            t.matmul(ps_d[n % 2][:], dsb[n % 2][:, 128 * ic:128 * (ic + 1)],
                                     hres[:, TB * ic:TB * (ic + 1)],
                                     start=(ic == 0), stop=False)
                        s_d.tag(t.matmul(ps_d[n % 2][:], bppd[:, 128 * oc:128 * (oc + 1)],
                                         ud_sb[:], start=False, stop=True))
                        if n >= 2:
                            t.wait_ge(s_mm6.h, n - 1)
                            if n > 3:
                                t.wait_ge(s_o6.h, n - 3)
                            for sblk in range(4):
                                ins = t.transpose(ps_t[(n - 1) % 2][:, 128 * sblk:128 * (sblk + 1)],
                                                  m_sb[(n - 1) % 2][:, 128 * sblk:128 * (sblk + 1)],
                                                  ident[:])
                            s_t6.tag(ins)
                t.wait_ge(s_mm6.h, NTOT)
                t.wait_ge(s_o6.h, NTOT - 2)
                for sblk in range(4):
                    ins = t.transpose(ps_t[NTOT % 2][:, 128 * sblk:128 * (sblk + 1)],
                                      m_sb[NTOT % 2][:, 128 * sblk:128 * (sblk + 1)], ident[:])
                s_t6.tag(ins)

            @block.scalar
            def _(sc):
                n = 0
                for tc in range(NTB):
                    sc.wait_ge(s_ud.h, tc + 1)
                    s_udev.tag(sc.copy(ud_sb[:], ps_ud[:]))
                    for oc in range(FC):
                        n += 1
                        sc.wait_ge(s_d.h, n)
                        if n > 2:
                            sc.wait_ge(s_mm6.h, n - 2)
                        s_g.tag(sc.activation(g_sb[n % 2][:], ps_d[n % 2][:], AF.Silu,
                                              scale=spcd[:, oc:oc + 1]))

            @block.vector
            def _(v):
                n = 0
                for tc in range(NTB):
                    for oc in range(FC):
                        n += 1
                        v.wait_ge(s_g.h, n)
                        s_sld.wait(v, n - 1)
                        if n > 2:
                            v.wait_ge(s_t6.h, n - 2)
                        s_mm6.tag(v.tensor_tensor(m_sb[n % 2][:], g_sb[n % 2][:], st_sb[n % 2][:], ALU.mult))
                        if n >= 2:
                            v.wait_ge(s_t6.h, n - 1)
                            s_xld.wait(v, n - 2)
                            if n > 3:
                                s_ost.wait(v, n - 4)
                            s_o6.tag(v.tensor_tensor(o_sb[(n - 1) % 2][:], ps_t[(n - 1) % 2][:],
                                                     x_sb6[(n - 1) % 2][:], ALU.add))
                v.wait_ge(s_t6.h, NTOT)
                s_xld.wait(v, NTOT - 1)
                s_ost.wait(v, NTOT - 3)
                s_o6.tag(v.tensor_tensor(o_sb[NTOT % 2][:], ps_t[NTOT % 2][:], x_sb6[NTOT % 2][:], ALU.add))


def stage_inputs(inputs):
    a = {k: np.asarray(v) for k, v in inputs.items()}
    common = {
        "wqT": np.ascontiguousarray(a["L0_attn_Wq"].T),
        "wkT": np.ascontiguousarray(a["L0_attn_Wk"].T),
        "wvT": np.ascontiguousarray(a["L0_attn_Wv"].T),
        "woT": np.ascontiguousarray(a["L0_attn_Wo"].T),
        "qgT": np.ascontiguousarray(a["L0_ffn_gate_q"].T),
        "quT": np.ascontiguousarray(a["L0_ffn_up_q"].T),
        "qdT": np.ascontiguousarray(a["L0_ffn_down_q"].T),
        "gsc_pc": np.ascontiguousarray(a["L0_ffn_gate_scale"].reshape(IC, 128).T),
        "usc_pc": np.ascontiguousarray(a["L0_ffn_up_scale"].reshape(IC, 128).T),
        "dsc_pc": np.ascontiguousarray(a["L0_ffn_down_scale"].reshape(FC, 128).T),
        "gsc_row": np.ascontiguousarray(a["L0_ffn_gate_scale"].reshape(1, IMED)),
        "usc_row": np.ascontiguousarray(a["L0_ffn_up_scale"].reshape(1, IMED)),
        "dsc_row": np.ascontiguousarray(a["L0_ffn_down_scale"].reshape(1, HID)),
        "gA": np.ascontiguousarray(a["L0_ffn_gate_A"]),
        "uA": np.ascontiguousarray(a["L0_ffn_up_A"]),
        "dA": np.ascontiguousarray(a["L0_ffn_down_A"]),
        "gB": np.ascontiguousarray(a["L0_ffn_gate_B"]),
        "uB": np.ascontiguousarray(a["L0_ffn_up_B"]),
        "dB": np.ascontiguousarray(a["L0_ffn_down_B"]),
    }
    x = a["L0_attn_hidden_states"]
    return [dict(common, x=np.ascontiguousarray(x[c])) for c in range(8)]


def kernel(**inputs):
    in_maps = stage_inputs(inputs)
    nc = build_program()
    res = run_bass_kernel_spmd(nc, in_maps, core_ids=list(range(8)))
    return np.stack([r["out"] for r in res.results]).astype(np.float32)


# version 04
# speedup vs baseline: 1.0178x; 1.0178x over previous
import numpy as np
import ml_dtypes
from contextlib import ExitStack

import concourse.bass as bass
import concourse.mybir as mybir
from concourse.bass_utils import run_bass_kernel_spmd

AF = mybir.ActivationFunctionType
ALU = mybir.AluOpType
F32 = mybir.dt.float32
BF16 = mybir.dt.bfloat16
I32 = mybir.dt.int32
AX = mybir.AxisListType.X

S_LEN = 2048
HID = 4096
HKV = 1024
IMED = 11008
NTT = 16
NTB = 4
TB = 512
FC = HID // 128
IC = IMED // 128
NH, NKV, DH = 32, 8, 128
EXP_SCALE = 1.0 / float(np.sqrt(128.0))
EXP_BIAS = -15.0
LN_EPS = 1e-5
SP_EPS = 1e-12
Q_ZERO, Q_STEP, LR = 7.5, 0.005, 0.02

ALL_PHASES = ("ln", "qkv", "gateup", "attn", "wo", "down")


def _host_consts():
    inv_freq = 1.0 / (10000.0 ** (np.arange(0, DH, 2, dtype=np.float32) / DH))
    t = np.arange(S_LEN, dtype=np.float32)
    freqs = np.outer(t, inv_freq)
    emb = np.concatenate([freqs, freqs], axis=1)
    cos_fm = np.ascontiguousarray(np.cos(emb).T.astype(np.float32))
    sin_fm = np.sin(emb).T.astype(np.float32)
    sinS_fm = np.ascontiguousarray(sin_fm * np.where(np.arange(DH)[:, None] < 64, -1.0, 1.0).astype(np.float32))
    ident = np.eye(128, dtype=ml_dtypes.bfloat16)
    prot = np.zeros((128, 128), dtype=ml_dtypes.bfloat16)
    prot[(np.arange(128) + 64) % 128, np.arange(128)] = 1
    masks = np.zeros((128, 4 * TB), dtype=ml_dtypes.bfloat16)
    jj = np.arange(128)[:, None]
    qq = np.arange(TB)[None, :]
    for r in range(4):
        masks[:, r * TB:(r + 1) * TB] = ((128 * r + jj) <= qq).astype(ml_dtypes.bfloat16)
    return ident, prot, cos_fm, sinS_fm, masks


def build_program(phases=ALL_PHASES, dbg=()):
    nc = bass.Bass("TRN2", target_bir_lowering=False)

    x_in = nc.dram_tensor("x", [S_LEN, HID], F32, kind="ExternalInput")
    wqT = nc.dram_tensor("wqT", [HID, HID], F32, kind="ExternalInput")
    wkT = nc.dram_tensor("wkT", [HID, HKV], F32, kind="ExternalInput")
    wvT = nc.dram_tensor("wvT", [HID, HKV], F32, kind="ExternalInput")
    woT = nc.dram_tensor("woT", [HID, HID], F32, kind="ExternalInput")
    qgT = nc.dram_tensor("qgT", [HID, IMED], I32, kind="ExternalInput")
    quT = nc.dram_tensor("quT", [HID, IMED], I32, kind="ExternalInput")
    qdT = nc.dram_tensor("qdT", [IMED, HID], I32, kind="ExternalInput")
    gsc_pc = nc.dram_tensor("gsc_pc", [128, IC], F32, kind="ExternalInput")
    usc_pc = nc.dram_tensor("usc_pc", [128, IC], F32, kind="ExternalInput")
    dsc_pc = nc.dram_tensor("dsc_pc", [128, FC], F32, kind="ExternalInput")
    gsc_row = nc.dram_tensor("gsc_row", [1, IMED], F32, kind="ExternalInput")
    usc_row = nc.dram_tensor("usc_row", [1, IMED], F32, kind="ExternalInput")
    dsc_row = nc.dram_tensor("dsc_row", [1, HID], F32, kind="ExternalInput")
    gA = nc.dram_tensor("gA", [HID, 32], F32, kind="ExternalInput")
    uA = nc.dram_tensor("uA", [HID, 32], F32, kind="ExternalInput")
    dA = nc.dram_tensor("dA", [IMED, 32], F32, kind="ExternalInput")
    gB = nc.dram_tensor("gB", [32, IMED], F32, kind="ExternalInput")
    uB = nc.dram_tensor("uB", [32, IMED], F32, kind="ExternalInput")
    dB = nc.dram_tensor("dB", [32, HID], F32, kind="ExternalInput")
    out = nc.dram_tensor("out", [S_LEN, HID], F32, kind="ExternalOutput")

    qT_s = nc.dram_tensor("qT_s", [HID, S_LEN], BF16)
    kT_s = nc.dram_tensor("kT_s", [HKV, S_LEN], BF16)
    vT_s = nc.dram_tensor("vT_s", [HKV, S_LEN], BF16)
    attn_s = nc.dram_tensor("attn_s", [HID, S_LEN], BF16)
    sT_s = nc.dram_tensor("sT_s", [HID, S_LEN], BF16)
    h_s = nc.dram_tensor("h_s", [IMED, S_LEN], BF16)
    dbf_s = nc.dram_tensor("dbf_s", [IMED, HID], BF16)

    ident_np, prot_np, cos_np, sinS_np, masks_np = _host_consts()
    ident_d = nc.inline_tensor(ident_np, "ident_c")
    prot_d = nc.inline_tensor(prot_np, "prot_c")
    cos_d = nc.inline_tensor(cos_np, "cos_c")
    sinS_d = nc.inline_tensor(sinS_np, "sinS_c")
    masks_d = nc.inline_tensor(masks_np, "masks_c")

    dbg_outs = {}
    for name, shape in (("bppg", [33, IMED]), ("bppu", [33, IMED]),
                        ("ugg", [33, S_LEN]), ("ugu", [33, S_LEN]),
                        ("xnT", [128, FC * S_LEN]), ("qT", [HID, S_LEN]),
                        ("kT", [HKV, S_LEN]), ("vT", [HKV, S_LEN]),
                        ("attn", [HID, S_LEN]), ("sT", [HID, S_LEN]),
                        ("h", [IMED, S_LEN]), ("dbf", [IMED, HID])):
        if name in dbg:
            dbg_outs[name] = nc.dram_tensor("dbg_" + name, shape, F32, kind="ExternalOutput")

    def slab_ap(dram, col0, ncols, nchunks):
        w = dram.shape[1]
        return bass.AP(dram, col0, [[w, 128], [128 * w, nchunks], [1, ncols]])

    def sb3(sbuf, off, d1, d2):
        row = int(np.prod(sbuf.shape[1:]))
        return bass.AP(sbuf, off, [[row, 128], list(d1), list(d2)])

    def dump_dram(scr, dram_out, sem):
        with nc.Block() as block:
            @block.gpsimd
            def _(g):
                g.dma_start(dram_out.ap(), scr.ap()).then_inc(sem.h, 16)
                g.wait_ge(sem.h, 16)

    stack_top = ExitStack()

    class Sem:
        __slots__ = ("h", "n")
        def __init__(self, name):
            self.h = stack_top.enter_context(nc.semaphore(name))
            self.n = 0
        def tag(self, instr, k=1):
            instr.then_inc(self.h, k)
            self.n += k
            return self.n

    class DSem:
        __slots__ = ("s",)
        def __init__(self, name):
            self.s = [Sem(name + "a"), Sem(name + "b")]
        def tag(self, i, instr):
            self.s[i % 2].tag(instr, 16)
        def wait(self, eng, i, per=1):
            eng.wait_ge(self.s[i % 2].h, 16 * per * (i // 2 + 1))
        def wait_all(self, eng):
            for sh in self.s:
                if sh.n:
                    eng.wait_ge(sh.h, sh.n)

    with stack_top:
        ident = stack_top.enter_context(nc.sbuf_tensor("ident", [128, 128], BF16))
        prot = stack_top.enter_context(nc.sbuf_tensor("prot", [128, 128], BF16))
        s_const = Sem("s_const")
        with nc.Block() as block:
            @block.sync
            def _(sy):
                s_const.tag(sy.dma_start(ident[:], ident_d.ap()), 16)
                s_const.tag(sy.dma_start(prot[:], prot_d.ap()), 16)
                sy.wait_ge(s_const.h, 32)

        s_pc = Sem("s_pc")
        precast_left = [86 if ("down" in phases) else 0]

        def emit_precast(g, k):
            for _ in range(k):
                if precast_left[0] == 0:
                    return
                ic = 86 - precast_left[0]
                s_pc.tag(g.dma_start(dbf_s.ap()[128 * ic:128 * (ic + 1), :],
                                     qdT.ap()[128 * ic:128 * (ic + 1), :]), 16)
                precast_left[0] -= 1

        with ExitStack() as stack_a:
            if "ln" in phases:
                xnT_res = stack_a.enter_context(nc.sbuf_tensor("xnT_res", [128, FC * S_LEN], BF16))
                _phase_ln(nc, Sem, DSem, x_in, xnT_res, ident)
                if "xnT" in dbg_outs:
                    sem = Sem("s_dbg_xnT")
                    with nc.Block() as block:
                        @block.gpsimd
                        def _(g):
                            g.dma_start(dbg_outs["xnT"].ap(), xnT_res[:]).then_inc(sem.h, 16)
                            g.wait_ge(sem.h, 16)

            if "qkv" in phases:
                _phase_qkv(nc, Sem, DSem, xnT_res, prot, cos_d, sinS_d,
                           wqT, wkT, wvT, qT_s, kT_s, vT_s, slab_ap, emit_precast)
                for nm, scr in (("qT", qT_s), ("kT", kT_s), ("vT", vT_s)):
                    if nm in dbg_outs:
                        dump_dram(scr, dbg_outs[nm], Sem("s_dbg_" + nm))

            if "gateup" in phases:
                _phase_gateup(nc, Sem, DSem, xnT_res, qgT, quT, gsc_pc, usc_pc,
                              gsc_row, usc_row, gA, uA, gB, uB, h_s, slab_ap, sb3, emit_precast,
                              dbg_sbufs={k: v for k, v in dbg_outs.items()
                                         if k in ("bppg", "bppu", "ugg", "ugu")} or None)
                if "h" in dbg_outs:
                    dump_dram(h_s, dbg_outs["h"], Sem("s_dbg_h"))

        if "attn" in phases:
            _phase_attn(nc, Sem, DSem, qT_s, kT_s, vT_s, attn_s, masks_d)
            if "attn" in dbg_outs:
                dump_dram(attn_s, dbg_outs["attn"], Sem("s_dbg_attn"))

        if "wo" in phases:
            _phase_wo(nc, Sem, DSem, woT, attn_s, sT_s, slab_ap)
            if "sT" in dbg_outs:
                dump_dram(sT_s, dbg_outs["sT"], Sem("s_dbg_sT"))

        if "dbf" in dbg_outs and "down" in phases:
            semd = Sem("s_dbg_dbf")
            with nc.Block() as block:
                @block.gpsimd
                def _(g):
                    emit_precast(g, 86)
                    g.wait_ge(s_pc.h, s_pc.n)
                    g.dma_start(dbg_outs["dbf"].ap(), dbf_s.ap()).then_inc(semd.h, 16)
                    g.wait_ge(semd.h, 16)

        if "down" in phases:
            _phase_down(nc, Sem, DSem, x_in, out, h_s, sT_s, dbf_s,
                        dsc_pc, dsc_row, dA, dB, ident, slab_ap, sb3,
                        emit_precast, s_pc)

    return nc


def _phase_ln(nc, Sem, DSem, x_in, xnT_res, ident):
    with ExitStack() as st:
        x_sb = [st.enter_context(nc.sbuf_tensor(f"x_sb{i}", [128, HID], F32)) for i in range(2)]
        xn_sb = [st.enter_context(nc.sbuf_tensor(f"xn_sb{i}", [128, HID], BF16)) for i in range(2)]
        trash = [st.enter_context(nc.sbuf_tensor(f"trash{i}", [128, HID], BF16)) for i in range(2)]
        stat = st.enter_context(nc.sbuf_tensor("stat", [128, 16], F32))
        ptr = [st.enter_context(nc.psum_tensor(f"ptr{i}", [128, 512], BF16)) for i in range(2)]
        s_x = DSem("s1_x")
        s_sq, s_var, s_rstd, s_xn, s_tr, s_evac = (
            Sem("s1_sq"), Sem("s1_var"),
            Sem("s1_rstd"), Sem("s1_xn"), Sem("s1_tr"), Sem("s1_evac"))

        def col(c):
            return stat[:, c:c + 1]

        def sb3l(sbuf, off, d1, d2):
            rw = int(np.prod(sbuf.shape[1:]))
            return bass.AP(sbuf, off, [[rw, 128], list(d1), list(d2)])

        with nc.Block() as block:
            @block.sync
            def _(sy):
                for i in range(NTT):
                    if i >= 2:
                        sy.wait_ge(s_xn.h, i - 1)
                    s_x.tag(i, sy.dma_start(x_sb[i % 2][:], x_in.ap()[128 * i:128 * (i + 1), :]))

            @block.scalar
            def _(sc):
                def rstd_pair(j):
                    b = j % 2
                    sc.wait_ge(s_var.h, j + 1)
                    if j >= 2:
                        sc.wait_ge(s_xn.h, j - 1)
                    sc.activation(col(14), col(10 + b), AF.Ln)
                    sc.activation(col(12 + b), col(14), AF.Exp, scale=-0.5)
                    s_rstd.tag(sc.copy(trash[b][:, 0:512], trash[b][:, 512:1024]))

                def evacs(jj):
                    for g in range(8):
                        G = 8 * jj + g
                        sc.wait_ge(s_tr.h, G + 1)
                        s_evac.tag(sc.copy(sb3l(xnT_res, (4 * g) * S_LEN + 128 * jj, (S_LEN, 4), (1, 128)),
                                           sb3l(ptr[G % 2], 0, (128, 4), (1, 128))))

                for i in range(NTT):
                    s_x.wait(sc, i)
                    if i >= 2:
                        sc.wait_ge(s_var.h, i - 1)
                        sc.wait_ge(s_rstd.h, i - 1)
                    s_sq.tag(sc.activation(trash[i % 2][:], x_sb[i % 2][:], AF.Square))
                    if i >= 1:
                        rstd_pair(i - 1)
                    if i >= 2:
                        evacs(i - 2)
                rstd_pair(NTT - 1)
                evacs(NTT - 2)
                evacs(NTT - 1)

            @block.vector
            def _(v):
                def stats_a(j):
                    b = j % 2
                    v.tensor_scalar(col(4 + b), col(0 + b), 1.0 / HID, None, ALU.mult)
                    v.tensor_scalar(col(6 + b), col(0 + b), col(0 + b), 1.0 / (HID * HID),
                                    ALU.mult, ALU.mult)
                    v.tensor_scalar(col(8 + b), col(2 + b), 1.0 / HID, LN_EPS, ALU.mult, ALU.add)

                def var_of(j):
                    b = j % 2
                    v.tensor_tensor(col(10 + b), col(8 + b), col(6 + b), ALU.subtract)

                def xn_of(j):
                    b = j % 2
                    v.wait_ge(s_rstd.h, j + 1)
                    if j >= 2:
                        v.wait_ge(s_tr.h, 8 * (j - 1))
                    s_xn.tag(v.tensor_scalar(xn_sb[b][:], x_sb[j % 2][:],
                                             col(4 + b), col(12 + b),
                                             ALU.subtract, ALU.mult))

                for i in range(NTT):
                    if i >= 1:
                        stats_a(i - 1)
                    s_x.wait(v, i)
                    v.reduce_sum(col(0 + i % 2), x_sb[i % 2][:], axis=AX)
                    if i >= 1:
                        var_of(i - 1)
                    v.wait_ge(s_sq.h, i + 1)
                    ins = v.reduce_sum(col(2 + i % 2), trash[i % 2][:], axis=AX)
                    if i >= 1:
                        s_var.tag(ins)
                    if i >= 1:
                        xn_of(i - 1)
                stats_a(NTT - 1)
                v.reduce_sum(col(15), x_sb[(NTT - 1) % 2][:], axis=AX)
                var_of(NTT - 1)
                s_var.tag(v.reduce_sum(col(15), x_sb[(NTT - 1) % 2][:], axis=AX))
                xn_of(NTT - 1)

            @block.tensor
            def _(t):
                for j in range(NTT):
                    t.wait_ge(s_xn.h, j + 1)
                    for g in range(8):
                        G = 8 * j + g
                        if G >= 2:
                            t.wait_ge(s_evac.h, G - 1)
                        for c4 in range(4):
                            c = 4 * g + c4
                            ins = t.transpose(ptr[G % 2][:, 128 * c4:128 * (c4 + 1)],
                                              xn_sb[j % 2][:, 128 * c:128 * (c + 1)], ident[:])
                        s_tr.tag(ins)


def _phase_qkv(nc, Sem, DSem, xnT_res, prot, cos_d, sinS_d,
               wqT, wkT, wvT, qT_s, kT_s, vT_s, slab_ap, emit_precast):
    with ExitStack() as st:
        cosf = st.enter_context(nc.sbuf_tensor("cosf", [128, S_LEN], F32))
        sinf = st.enter_context(nc.sbuf_tensor("sinf", [128, S_LEN], F32))
        wsl = [st.enter_context(nc.sbuf_tensor(f"wsl{i}", [128, FC * 128], BF16)) for i in range(2)]
        raw_bf = [st.enter_context(nc.sbuf_tensor(f"raw{i}", [128, S_LEN], BF16)) for i in range(2)]
        rope_sb = [st.enter_context(nc.sbuf_tensor(f"rope{i}", [128, S_LEN], BF16)) for i in range(2)]
        tmp1 = st.enter_context(nc.sbuf_tensor("tmp1", [128, TB], F32))
        tmp2 = st.enter_context(nc.sbuf_tensor("tmp2", [128, TB], F32))
        psp = [st.enter_context(nc.psum_tensor(f"psp{i}", [128, 4 * TB], F32)) for i in range(2)]
        s_ct = Sem("s2_ct")
        s_wld = DSem("s2_wld")
        s_proj = Sem("s2_proj")
        s_raw = Sem("s2_raw")
        s_rot = Sem("s2_rot")
        s_rope = Sem("s2_rope")
        s_store = DSem("s2_store")

        chunks = ([("q", dc) for dc in range(FC)] + [("k", dc) for dc in range(8)]
                  + [("v", dc) for dc in range(8)])
        NCH = len(chunks)
        wsrc = {"q": wqT, "k": wkT, "v": wvT}
        dsts = {"q": qT_s, "k": kT_s, "v": vT_s}

        with nc.Block() as block:
            @block.sync
            def _(sy):
                s_ct.tag(sy.dma_start(cosf[:], cos_d.ap()), 16)
                s_ct.tag(sy.dma_start(sinf[:], sinS_d.ap()), 16)
                for j, (kind, dc) in enumerate(chunks):
                    if kind == "v":
                        sy.wait_ge(s_raw.h, 4 * (j + 1))
                        src = raw_bf[j % 2]
                    else:
                        sy.wait_ge(s_rope.h, 4 * (j + 1))
                        src = rope_sb[j % 2]
                    s_store.tag(j, sy.dma_start(dsts[kind].ap()[128 * dc:128 * (dc + 1), :], src[:]))
                s_store.wait_all(sy)

            @block.gpsimd
            def _(g):
                for j, (kind, dc) in enumerate(chunks):
                    if j >= 2:
                        g.wait_ge(s_proj.h, j - 1)
                    s_wld.tag(j, g.dma_start(wsl[j % 2][:], slab_ap(wsrc[kind], 128 * dc, 128, FC)))
                    emit_precast(g, 2)

            @block.tensor
            def _(t):
                def emit_rot(i):
                    t.wait_ge(s_raw.h, 4 * (i + 1))
                    for tb in range(NTB):
                        s_rot.tag(t.matmul(psp[i % 2][:, TB * tb:TB * (tb + 1)], prot[:],
                                           raw_bf[i % 2][:, TB * tb:TB * (tb + 1)],
                                           start=True, stop=True))

                for j, (kind, dc) in enumerate(chunks):
                    s_wld.wait(t, j)
                    if j >= 2:
                        if chunks[j - 2][0] == "v":
                            t.wait_ge(s_raw.h, 4 * (j - 1))
                        else:
                            t.wait_ge(s_rope.h, 4 * (j - 1))
                    for f in range(FC):
                        for tb in range(NTB):
                            ins = t.matmul(psp[j % 2][:, TB * tb:TB * (tb + 1)],
                                           wsl[j % 2][:, 128 * f:128 * (f + 1)],
                                           xnT_res[:, S_LEN * f + TB * tb:S_LEN * f + TB * (tb + 1)],
                                           start=(f == 0), stop=(f == FC - 1))
                    s_proj.tag(ins)
                    if j >= 1 and chunks[j - 1][0] != "v":
                        emit_rot(j - 1)
                if chunks[NCH - 1][0] != "v":
                    emit_rot(NCH - 1)

            @block.scalar
            def _(sc):
                for j, (kind, dc) in enumerate(chunks):
                    sc.wait_ge(s_proj.h, j + 1)
                    if j >= 2:
                        s_store.wait(sc, j - 2)
                        if chunks[j - 2][0] != "v":
                            sc.wait_ge(s_rope.h, 4 * (j - 1))
                    for tb in range(NTB):
                        s_raw.tag(sc.copy(raw_bf[j % 2][:, TB * tb:TB * (tb + 1)],
                                          psp[j % 2][:, TB * tb:TB * (tb + 1)]))

            @block.vector
            def _(v):
                v.wait_ge(s_ct.h, 32)
                for j, (kind, dc) in enumerate(chunks):
                    if kind == "v":
                        continue
                    for tb in range(NTB):
                        v.wait_ge(s_raw.h, 4 * j + tb + 1)
                        if tb == 0 and j >= 2:
                            s_store.wait(v, j - 2)
                        v.tensor_tensor(tmp1[:], raw_bf[j % 2][:, TB * tb:TB * (tb + 1)],
                                        cosf[:, TB * tb:TB * (tb + 1)], ALU.mult)
                        v.wait_ge(s_rot.h, 4 * j + tb + 1)
                        v.tensor_tensor(tmp2[:], psp[j % 2][:, TB * tb:TB * (tb + 1)],
                                        sinf[:, TB * tb:TB * (tb + 1)], ALU.mult)
                        s_rope.tag(v.tensor_tensor(rope_sb[j % 2][:, TB * tb:TB * (tb + 1)],
                                                   tmp1[:], tmp2[:], ALU.add))


def _phase_gateup(nc, Sem, DSem, xnT_res, qgT, quT, gsc_pc, usc_pc,
                  gsc_row, usc_row, gA, uA, gB, uB, h_s, slab_ap, sb3, emit_precast,
                  dbg_sbufs=None):
    with ExitStack() as st:
        bpp = [st.enter_context(nc.sbuf_tensor(f"bpp{i}", [33, IMED], BF16)) for i in range(2)]
        ug_sb = [st.enter_context(nc.sbuf_tensor(f"ug{i}", [33, S_LEN], BF16)) for i in range(2)]
        spc = [st.enter_context(nc.sbuf_tensor(f"spc{i}", [128, IC], F32)) for i in range(2)]
        sg_sb = [st.enter_context(nc.sbuf_tensor(f"sg{i}", [128, TB], BF16)) for i in range(4)]
        uu_sb = [st.enter_context(nc.sbuf_tensor(f"uu{i}", [128, TB], BF16)) for i in range(4)]
        h_sb = [st.enter_context(nc.sbuf_tensor(f"hsb{i}", [128, TB], BF16)) for i in range(2)]

        s_pl = Sem("s5_pl")
        s_pg = Sem("s5_pg")
        s_umm = Sem("s5_umm")
        s_uev = Sem("s5_uev")
        s_bld = DSem("s5_bld")
        s_rsc = Sem("s5_rsc")
        s_bmm = Sem("s5_bmm")
        s_btt = Sem("s5_btt")
        s_wg = Sem("s5_wg")
        s_wu = Sem("s5_wu")
        s_gp = Sem("s5_gp")
        s_up = Sem("s5_up")
        s_sil = Sem("s5_sil")
        s_cpy = Sem("s5_cpy")
        s_h = Sem("s5_h")
        s_hd = DSem("s5_hd")

        with ExitStack() as pre:
            aaug = [pre.enter_context(nc.sbuf_tensor(f"aaug{i}", [128, FC * 33], BF16)) for i in range(2)]
            sclf = [pre.enter_context(nc.sbuf_tensor(f"sclf{i}", [1, TB], F32)) for i in range(2)]
            sclm = pre.enter_context(nc.sbuf_tensor("sclm", [1, TB], F32))
            rscb = [pre.enter_context(nc.sbuf_tensor(f"rscb{i}", [1, TB], BF16)) for i in range(2)]
            bst = [pre.enter_context(nc.sbuf_tensor(f"bst{i}", [32, TB], F32)) for i in range(2)]
            one1x32 = pre.enter_context(nc.sbuf_tensor("one1x32", [1, 32], BF16))
            ps_a = pre.enter_context(nc.psum_tensor("ps_a", [33, TB], F32))
            ps_bc = pre.enter_context(nc.psum_tensor("ps_bc", [32, TB], F32))

            NCHB = (IMED + TB - 1) // TB
            chw = [min(TB, IMED - TB * c) for c in range(NCHB)]
            s_ms = Sem("s5_ms")
            with nc.Block() as block:
                @block.sync
                def _(sy):
                    s_pl.tag(sy.dma_start(spc[0][:], gsc_pc.ap()), 16)
                    s_pl.tag(sy.dma_start(spc[1][:], usc_pc.ap()), 16)
                    for gg, scrow in ((0, gsc_row), (1, usc_row)):
                        Bsrc = gB if gg == 0 else uB
                        for ch in range(NCHB):
                            gi = gg * NCHB + ch
                            w = chw[ch]
                            if gi >= 2:
                                sy.wait_ge(s_btt.h, gi - 1)
                            s_bld.tag(gi, sy.dma_start(sclf[gi % 2][:, 0:w], scrow.ap()[:, TB * ch:TB * ch + w]))
                            s_bld.tag(gi, sy.dma_start(bst[gi % 2][:, 0:w], Bsrc.ap()[:, TB * ch:TB * ch + w]))

                @block.gpsimd
                def _(g):
                    s_pg.tag(g.dma_start(sb3(aaug[0], 0, (33, FC), (1, 32)),
                                         bass.AP(gA, 0, [[32, 128], [128 * 32, FC], [1, 32]])), 16)
                    s_pg.tag(g.dma_start(sb3(aaug[1], 0, (33, FC), (1, 32)),
                                         bass.AP(uA, 0, [[32, 128], [128 * 32, FC], [1, 32]])), 16)
                    emit_precast(g, 3)

                @block.vector
                def _(v):
                    v.memset(one1x32[:], 1.0)
                    v.wait_ge(s_pg.h, 32)
                    for gg in range(2):
                        v.memset(bass.AP(aaug[gg], 32, [[FC * 33, 128], [33, FC]]), 1.0)
                    s_ms.tag(v.memset(one1x32[:], 1.0))
                    v.wait_ge(s_pl.h, 32)
                    for gg in range(2):
                        v.tensor_scalar(spc[gg][:], spc[gg][:], Q_STEP, SP_EPS, ALU.mult, ALU.max)
                    for gg in range(2):
                        for ch in range(NCHB):
                            gi = gg * NCHB + ch
                            w = chw[ch]
                            s_bld.wait(v, gi, per=2)
                            v.tensor_scalar(sclm[:, 0:w], sclf[gi % 2][:, 0:w], Q_STEP, SP_EPS, ALU.mult, ALU.max)
                            v.reciprocal(sclm[:, 0:w], sclm[:, 0:w])
                            s_rsc.tag(v.tensor_scalar(rscb[gi % 2][:, 0:w], sclm[:, 0:w], LR, None, ALU.mult))
                            v.wait_ge(s_bmm.h, gi + 1)
                            s_btt.tag(v.tensor_tensor(bpp[gg][0:32, TB * ch:TB * ch + w],
                                                      bst[gi % 2][:, 0:w], ps_bc[:, 0:w], ALU.mult))
                        v.memset(bpp[gg][32:33, :], -Q_ZERO)

                @block.tensor
                def _(t):
                    t.wait_ge(s_pg.h, 32)
                    t.wait_ge(s_ms.h, 1)
                    for gg in range(2):
                        for tb in range(NTB):
                            gi = 4 * gg + tb
                            if gi >= 1:
                                t.wait_ge(s_uev.h, gi)
                            for f in range(FC):
                                ins = t.matmul(ps_a[:], aaug[gg][:, 33 * f:33 * (f + 1)],
                                               xnT_res[:, S_LEN * f + TB * tb:S_LEN * f + TB * (tb + 1)],
                                               start=(f == 0), stop=(f == FC - 1))
                            s_umm.tag(ins)
                    for gg in range(2):
                        for ch in range(NCHB):
                            gi = gg * NCHB + ch
                            w = chw[ch]
                            t.wait_ge(s_rsc.h, gi + 1)
                            if gi >= 1:
                                t.wait_ge(s_btt.h, gi)
                            s_bmm.tag(t.matmul(ps_bc[:, 0:w], one1x32[:], rscb[gi % 2][:, 0:w],
                                               start=True, stop=True))

                @block.scalar
                def _(sc):
                    for gi in range(8):
                        sc.wait_ge(s_umm.h, gi + 1)
                        gg, tb = gi // 4, gi % 4
                        s_uev.tag(sc.copy(ug_sb[gg][:, TB * tb:TB * (tb + 1)], ps_a[:]))

        gsl = st.enter_context(nc.sbuf_tensor("gsl", [128, FC * 128], BF16))
        usl = st.enter_context(nc.sbuf_tensor("usl", [128, FC * 128], BF16))
        ps_g = [stackless_psum(nc, st, f"psg{i}", [128, 2 * TB]) for i in range(2)]
        ps_u = [stackless_psum(nc, st, f"psu{i}", [128, 2 * TB]) for i in range(2)]

        def psbank(ps, tb):
            return ps[tb // 2][:, TB * (tb % 2):TB * (tb % 2 + 1)]

        with nc.Block() as block:
            @block.gpsimd
            def _(g):
                for oc in range(IC):
                    if oc >= 1:
                        g.wait_ge(s_gp.h, oc)
                    s_wg.tag(g.dma_start(gsl[:], slab_ap(qgT, 128 * oc, 128, FC)), 16)
                    emit_precast(g, 1)
                    if oc >= 1:
                        g.wait_ge(s_up.h, oc)
                    s_wu.tag(g.dma_start(usl[:], slab_ap(quT, 128 * oc, 128, FC)), 16)

            @block.tensor
            def _(t):
                for oc in range(IC):
                    t.wait_ge(s_wg.h, 16 * (oc + 1))
                    if oc >= 1:
                        t.wait_ge(s_sil.h, 4 * oc)
                    for f in range(FC):
                        for tb in range(NTB):
                            t.matmul(psbank(ps_g, tb),
                                     gsl[:, 128 * f:128 * (f + 1)],
                                     xnT_res[:, S_LEN * f + TB * tb:S_LEN * f + TB * (tb + 1)],
                                     start=(f == 0), stop=False)
                    for tb in range(NTB):
                        last = t.matmul(psbank(ps_g, tb),
                                        bpp[0][:, 128 * oc:128 * (oc + 1)],
                                        ug_sb[0][:, TB * tb:TB * (tb + 1)],
                                        start=False, stop=True)
                    s_gp.tag(last)
                    t.wait_ge(s_wu.h, 16 * (oc + 1))
                    if oc >= 1:
                        t.wait_ge(s_cpy.h, 4 * oc)
                    for f in range(FC):
                        for tb in range(NTB):
                            t.matmul(psbank(ps_u, tb),
                                     usl[:, 128 * f:128 * (f + 1)],
                                     xnT_res[:, S_LEN * f + TB * tb:S_LEN * f + TB * (tb + 1)],
                                     start=(f == 0), stop=False)
                    for tb in range(NTB):
                        last = t.matmul(psbank(ps_u, tb),
                                        bpp[1][:, 128 * oc:128 * (oc + 1)],
                                        ug_sb[1][:, TB * tb:TB * (tb + 1)],
                                        start=False, stop=True)
                    s_up.tag(last)

            @block.scalar
            def _(sc):
                cnt = 0
                for oc in range(IC):
                    sc.wait_ge(s_gp.h, oc + 1)
                    for tb in range(NTB):
                        cnt += 1
                        if cnt > 4:
                            sc.wait_ge(s_h.h, cnt - 4)
                        s_sil.tag(sc.activation(sg_sb[cnt % 4][:], psbank(ps_g, tb),
                                                AF.Silu, scale=spc[0][:, oc:oc + 1]))
                    sc.wait_ge(s_up.h, oc + 1)
                    for tb in range(NTB):
                        cnt2 = 4 * oc + tb + 1
                        if cnt2 > 4:
                            sc.wait_ge(s_h.h, cnt2 - 4)
                        s_cpy.tag(sc.activation(uu_sb[cnt2 % 4][:], psbank(ps_u, tb),
                                                AF.Copy, scale=spc[1][:, oc:oc + 1]))

            @block.vector
            def _(v):
                cnt = 0
                for oc in range(IC):
                    for tb in range(NTB):
                        cnt += 1
                        v.wait_ge(s_cpy.h, cnt)
                        if cnt > 2:
                            s_hd.wait(v, cnt - 3)
                        s_h.tag(v.tensor_tensor(h_sb[cnt % 2][:], sg_sb[cnt % 4][:], uu_sb[cnt % 4][:], ALU.mult))

            @block.sync
            def _(sy):
                cnt = 0
                for oc in range(IC):
                    for tb in range(NTB):
                        cnt += 1
                        sy.wait_ge(s_h.h, cnt)
                        s_hd.tag(cnt - 1, sy.dma_start(h_s.ap()[128 * oc:128 * (oc + 1), TB * tb:TB * (tb + 1)],
                                                       h_sb[cnt % 2][:]))
                s_hd.wait_all(sy)


def stackless_psum(nc, st, name, shape):
    return st.enter_context(nc.psum_tensor(name, shape, F32))


def _phase_attn(nc, Sem, DSem, qT_s, kT_s, vT_s, attn_s, masks_d):
    with ExitStack() as st:
        kres = st.enter_context(nc.sbuf_tensor("kres", [128, NKV * S_LEN], BF16))
        vres = st.enter_context(nc.sbuf_tensor("vres", [128, NTT * HKV], BF16))
        maskr = st.enter_context(nc.sbuf_tensor("maskr", [128, 4 * TB], BF16))
        qt_sb = [st.enter_context(nc.sbuf_tensor(f"qt{i}", [128, TB], BF16)) for i in range(2)]
        et_sb = [st.enter_context(nc.sbuf_tensor(f"et{i}", [128, TB], BF16)) for i in range(4)]
        etm_sb = [st.enter_context(nc.sbuf_tensor(f"etm{i}", [128, TB], BF16)) for i in range(4)]
        rz_f = [st.enter_context(nc.sbuf_tensor(f"rzf{i}", [1, TB], F32)) for i in range(2)]
        rz_bf = [st.enter_context(nc.sbuf_tensor(f"rzb{i}", [1, TB], BF16)) for i in range(2)]
        rzb_sb = [st.enter_context(nc.sbuf_tensor(f"rzbs{i}", [128, TB], BF16)) for i in range(2)]
        ao_sb = [st.enter_context(nc.sbuf_tensor(f"ao{i}", [128, TB], BF16)) for i in range(2)]
        zacc = [st.enter_context(nc.sbuf_tensor(f"zacc{i}", [128, TB], BF16)) for i in range(2)]
        ones_z = st.enter_context(nc.sbuf_tensor("ones_z", [128, 1], BF16))
        ones_r = st.enter_context(nc.sbuf_tensor("ones_r", [1, 128], BF16))
        ebias = st.enter_context(nc.sbuf_tensor("ebias", [128, 1], F32))
        ps_s = [st.enter_context(nc.psum_tensor(f"pss{i}", [128, TB], F32)) for i in range(4)]
        ps_av = [st.enter_context(nc.psum_tensor(f"psav{i}", [128, TB], F32)) for i in range(2)]
        ps_z = [st.enter_context(nc.psum_tensor(f"psz{i}", [128, TB], F32)) for i in range(2)]

        s_res = Sem("s3_res")
        s_qld = DSem("s3_qld")
        s_s = Sem("s3_s")
        s_e = Sem("s3_e")
        s_m = Sem("s3_m")
        s_av = Sem("s3_av")
        s_rz = Sem("s3_rz")
        s_b = Sem("s3_b")
        s_rzb = Sem("s3_rzb")
        s_attn = Sem("s3_attn")
        s_zac = Sem("s3_zac")
        s_zmm = Sem("s3_zmm")
        s_ast = DSem("s3_ast")

        heads = [(Q, h) for Q in range(NTB) for h in range(NH)]
        NHD = len(heads)
        chunk_no = {}
        exp_waits = {}
        msk_cum = [0]
        head_last = []
        c = 0
        m = 0
        for idx, (Q, h) in enumerate(heads):
            for j in range(4 * (Q + 1)):
                c += 1
                chunk_no[(idx, j)] = c
                if j >= 4 * Q:
                    m += 1
                    exp_waits[(idx, j)] = ("m", m)
                else:
                    exp_waits[(idx, j)] = ("e", c)
                msk_cum.append(m)
            head_last.append(c)
        NC_TOT = c

        with nc.Block() as block:
            @block.vector
            def _(v):
                v.memset(ones_z[:], 1.0)
                v.memset(ones_r[:], 1.0)
                v.memset(ebias[:], EXP_BIAS)

        with nc.Block() as block:
            @block.sync
            def _(sy):
                for g in range(NKV):
                    s_res.tag(sy.dma_start(kres[:, S_LEN * g:S_LEN * (g + 1)],
                                           kT_s.ap()[128 * g:128 * (g + 1), :]), 16)
                s_res.tag(sy.dma_start(maskr[:], masks_d.ap()), 16)
                for jc in range(NTT):
                    for g in range(NKV):
                        s_res.tag(sy.dma_start_transpose(
                            vres[:, HKV * jc + 128 * g:HKV * jc + 128 * (g + 1)],
                            vT_s.ap()[128 * g:128 * (g + 1), 128 * jc:128 * (jc + 1)]), 16)
                for idx in range(NHD):
                    Q, h = heads[idx]
                    if idx >= 2:
                        sy.wait_ge(s_s.h, head_last[idx - 2])
                    s_qld.tag(idx, sy.dma_start(qt_sb[idx % 2][:],
                                                qT_s.ap()[128 * h:128 * (h + 1), TB * Q:TB * (Q + 1)]))
                    if idx >= 1:
                        Qp, hp = heads[idx - 1]
                        sy.wait_ge(s_attn.h, idx)
                        s_ast.tag(idx - 1, sy.dma_start(attn_s.ap()[128 * hp:128 * (hp + 1), TB * Qp:TB * (Qp + 1)],
                                                        ao_sb[(idx - 1) % 2][:]))
                Qp, hp = heads[NHD - 1]
                sy.wait_ge(s_attn.h, NHD)
                s_ast.tag(NHD - 1, sy.dma_start(attn_s.ap()[128 * hp:128 * (hp + 1), TB * Qp:TB * (Qp + 1)],
                                                ao_sb[(NHD - 1) % 2][:]))
                s_ast.wait_all(sy)

            @block.vector
            def _(v):
                for idx in range(NHD):
                    Q, h = heads[idx]
                    n_j = 4 * (Q + 1)
                    for j in range(n_j):
                        cc = chunk_no[(idx, j)]
                        v.wait_ge(s_e.h, cc)
                        diag = j >= 4 * Q
                        if diag:
                            if cc > 4:
                                v.wait_ge(s_av.h, cc - 4)
                            r = j - 4 * Q
                            s_m.tag(v.tensor_tensor(etm_sb[cc % 4][:], et_sb[cc % 4][:],
                                                    maskr[:, TB * r:TB * (r + 1)], ALU.mult))
                            esrc = etm_sb[cc % 4]
                        else:
                            esrc = et_sb[cc % 4]
                        if j == 0:
                            if idx >= 2:
                                v.wait_ge(s_zmm.h, idx - 1)
                            s_zac.tag(v.tensor_copy(zacc[idx % 2][:], esrc[:]))
                        else:
                            s_zac.tag(v.tensor_tensor(zacc[idx % 2][:], zacc[idx % 2][:], esrc[:], ALU.add))
                    v.wait_ge(s_zmm.h, idx + 1)
                    v.reciprocal(rz_f[idx % 2][:], ps_z[idx % 2][0:1, :])
                    s_rz.tag(v.tensor_copy(rz_bf[idx % 2][:], rz_f[idx % 2][:]))
                    v.wait_ge(s_rzb.h, idx + 1)
                    if idx >= 2:
                        s_ast.wait(v, idx - 2)
                    v.wait_ge(s_av.h, head_last[idx])
                    s_attn.tag(v.tensor_tensor(ao_sb[idx % 2][:], ps_av[idx % 2][:],
                                               rzb_sb[idx % 2][:], ALU.mult))

            @block.scalar
            def _(sc):
                for idx in range(NHD):
                    Q, h = heads[idx]
                    for j in range(4 * (Q + 1)):
                        cc = chunk_no[(idx, j)]
                        sc.wait_ge(s_s.h, cc)
                        if cc > 4:
                            sc.wait_ge(s_av.h, cc - 4)
                            sc.wait_ge(s_zac.h, cc - 4)
                        s_e.tag(sc.activation(et_sb[cc % 4][:], ps_s[(cc - 1) % 4][:], AF.Exp,
                                              scale=EXP_SCALE, bias=ebias[:, 0:1]))
                    sc.wait_ge(s_b.h, idx + 1)
                    s_rzb.tag(sc.copy(rzb_sb[idx % 2][:], ps_z[idx % 2][:]))

            @block.tensor
            def _(t):
                t.wait_ge(s_res.h, s_res.n)

                def emit_avz(idx, j, n_j):
                    Q, h = heads[idx]
                    g = h // 4
                    cc = chunk_no[(idx, j)]
                    kind, val = exp_waits[(idx, j)]
                    t.wait_ge(s_m.h if kind == "m" else s_e.h, val)
                    esrc = etm_sb[cc % 4] if kind == "m" else et_sb[cc % 4]
                    s_av.tag(t.matmul(ps_av[idx % 2][:],
                                      vres[:, HKV * j + 128 * g:HKV * j + 128 * (g + 1)],
                                      esrc[:], start=(j == 0), stop=(j == n_j - 1)))

                for idx in range(NHD):
                    Q, h = heads[idx]
                    g = h // 4
                    n_j = 4 * (Q + 1)
                    s_qld.wait(t, idx)
                    if idx >= 2:
                        t.wait_ge(s_attn.h, idx - 1)
                    for j in range(n_j):
                        cc = chunk_no[(idx, j)]
                        if cc > 4:
                            t.wait_ge(s_e.h, cc - 4)
                        s_s.tag(t.matmul(ps_s[(cc - 1) % 4][:],
                                         kres[:, S_LEN * g + 128 * j:S_LEN * g + 128 * (j + 1)],
                                         qt_sb[idx % 2][:], start=True, stop=True))
                        if j >= 3:
                            emit_avz(idx, j - 3, n_j)
                    for jt in range(max(0, n_j - 3), n_j):
                        emit_avz(idx, jt, n_j)
                    t.wait_ge(s_zac.h, head_last[idx])
                    if idx >= 2:
                        t.wait_ge(s_rzb.h, idx - 1)
                    s_zmm.tag(t.matmul(ps_z[idx % 2][0:1, :], ones_z[:], zacc[idx % 2][:],
                                       start=True, stop=True))
                    t.wait_ge(s_rz.h, idx + 1)
                    s_b.tag(t.matmul(ps_z[idx % 2][:], ones_r[:], rz_bf[idx % 2][:],
                                     start=True, stop=True))


def _phase_wo(nc, Sem, DSem, woT, attn_s, sT_s, slab_ap):
    with ExitStack() as st:
        ares = st.enter_context(nc.sbuf_tensor("ares", [128, FC * S_LEN], BF16))
        wsl4 = [st.enter_context(nc.sbuf_tensor(f"wsl4{i}", [128, FC * 128], BF16)) for i in range(2)]
        s_sb4 = [st.enter_context(nc.sbuf_tensor(f"ssb4{i}", [128, S_LEN], BF16)) for i in range(2)]
        ps4 = [st.enter_context(nc.psum_tensor(f"ps4{i}", [128, 4 * TB], F32)) for i in range(2)]
        s_ares = Sem("s4_ares")
        s_wld4 = DSem("s4_wld")
        s_p4 = Sem("s4_p")
        s_ev4 = Sem("s4_ev")
        s_st4 = DSem("s4_st")

        with nc.Block() as block:
            @block.sync
            def _(sy):
                for f in range(FC):
                    s_ares.tag(sy.dma_start(ares[:, S_LEN * f:S_LEN * (f + 1)],
                                            attn_s.ap()[128 * f:128 * (f + 1), :]), 16)
                for oc in range(FC):
                    sy.wait_ge(s_ev4.h, 4 * (oc + 1))
                    s_st4.tag(oc, sy.dma_start(sT_s.ap()[128 * oc:128 * (oc + 1), :], s_sb4[oc % 2][:]))
                s_st4.wait_all(sy)

            @block.gpsimd
            def _(g):
                for oc in range(FC):
                    if oc >= 2:
                        g.wait_ge(s_p4.h, oc - 1)
                    s_wld4.tag(oc, g.dma_start(wsl4[oc % 2][:], slab_ap(woT, 128 * oc, 128, FC)))

            @block.tensor
            def _(t):
                for oc in range(FC):
                    s_wld4.wait(t, oc)
                    if oc == 0:
                        t.wait_ge(s_ares.h, 16 * FC)
                    if oc >= 2:
                        t.wait_ge(s_ev4.h, 4 * (oc - 1))
                    for f in range(FC):
                        for tb in range(NTB):
                            ins = t.matmul(ps4[oc % 2][:, TB * tb:TB * (tb + 1)],
                                           wsl4[oc % 2][:, 128 * f:128 * (f + 1)],
                                           ares[:, S_LEN * f + TB * tb:S_LEN * f + TB * (tb + 1)],
                                           start=(f == 0), stop=(f == FC - 1))
                    s_p4.tag(ins)

            @block.scalar
            def _(sc):
                for oc in range(FC):
                    sc.wait_ge(s_p4.h, oc + 1)
                    if oc >= 2:
                        s_st4.wait(sc, oc - 2)
                    for tb in range(NTB):
                        s_ev4.tag(sc.copy(s_sb4[oc % 2][:, TB * tb:TB * (tb + 1)],
                                          ps4[oc % 2][:, TB * tb:TB * (tb + 1)]))


def _phase_down(nc, Sem, DSem, x_in, out, h_s, sT_s, dbf_s,
                dsc_pc, dsc_row, dA, dB, ident, slab_ap, sb3, emit_precast, s_pc):
    with ExitStack() as st:
        hres = st.enter_context(nc.sbuf_tensor("hres", [128, IC * TB], BF16))
        dsb = [st.enter_context(nc.sbuf_tensor(f"dsb{i}", [128, IC * 128], BF16)) for i in range(2)]
        aaugd = st.enter_context(nc.sbuf_tensor("aaugd", [128, IC * 33], BF16))
        bppd = st.enter_context(nc.sbuf_tensor("bppd", [33, HID], BF16))
        ud_sb = st.enter_context(nc.sbuf_tensor("ud_sb", [33, TB], BF16))
        spcd = st.enter_context(nc.sbuf_tensor("spcd", [128, FC], F32))
        g_sb = [st.enter_context(nc.sbuf_tensor(f"g6{i}", [128, TB], BF16)) for i in range(2)]
        m_sb = [st.enter_context(nc.sbuf_tensor(f"m6{i}", [128, TB], BF16)) for i in range(2)]
        st_sb = [st.enter_context(nc.sbuf_tensor(f"st6{i}", [128, TB], BF16)) for i in range(2)]
        x_sb6 = [st.enter_context(nc.sbuf_tensor(f"x6{i}", [128, TB], F32)) for i in range(2)]
        o_sb = [st.enter_context(nc.sbuf_tensor(f"o6{i}", [128, TB], F32)) for i in range(2)]

        s_pl6 = Sem("s6_pl")
        s_bl6 = DSem("s6_bl")
        s_pg6 = Sem("s6_pg")
        s_rsc6 = Sem("s6_rsc")
        s_bmm6 = Sem("s6_bmm")
        s_btt6 = Sem("s6_btt")
        s_hld = [DSem(f"s6_hld{p}") for p in range(4)]
        s_dsl = DSem("s6_dsl")
        s_ud = Sem("s6_ud")
        s_udev = Sem("s6_udev")
        s_d = Sem("s6_d")
        s_g = Sem("s6_g")
        s_sld = DSem("s6_sld")
        s_xld = DSem("s6_xld")
        s_mm6 = Sem("s6_m")
        s_t6 = Sem("s6_t")
        s_o6 = Sem("s6_o")
        s_ost = DSem("s6_ost")

        with ExitStack() as pre:
            sclf = [pre.enter_context(nc.sbuf_tensor(f"sclf6{i}", [1, TB], F32)) for i in range(2)]
            sclm = pre.enter_context(nc.sbuf_tensor("sclm6", [1, TB], F32))
            rscb = [pre.enter_context(nc.sbuf_tensor(f"rscb6{i}", [1, TB], BF16)) for i in range(2)]
            bstd = [pre.enter_context(nc.sbuf_tensor(f"bstd{i}", [32, TB], F32)) for i in range(2)]
            one1x32d = pre.enter_context(nc.sbuf_tensor("one1x32d", [1, 32], BF16))
            ps_bcd = pre.enter_context(nc.psum_tensor("ps_bcd", [32, TB], F32))
            NCHB = HID // TB

            with nc.Block() as block:
                @block.sync
                def _(sy):
                    s_pl6.tag(sy.dma_start(spcd[:], dsc_pc.ap()), 16)
                    for ch in range(NCHB):
                        if ch >= 2:
                            sy.wait_ge(s_btt6.h, ch - 1)
                        s_bl6.tag(ch, sy.dma_start(sclf[ch % 2][:], dsc_row.ap()[:, TB * ch:TB * (ch + 1)]))
                        s_bl6.tag(ch, sy.dma_start(bstd[ch % 2][:], dB.ap()[:, TB * ch:TB * (ch + 1)]))

                @block.gpsimd
                def _(g):
                    emit_precast(g, 86)
                    s_pg6.tag(g.dma_start(sb3(aaugd, 0, (33, IC), (1, 32)),
                                          bass.AP(dA, 0, [[32, 128], [128 * 32, IC], [1, 32]])), 16)

                @block.vector
                def _(v):
                    v.memset(one1x32d[:], 1.0)
                    v.wait_ge(s_pg6.h, 16)
                    v.memset(bass.AP(aaugd, 32, [[IC * 33, 128], [33, IC]]), 1.0)
                    v.wait_ge(s_pl6.h, 16)
                    v.tensor_scalar(spcd[:], spcd[:], Q_STEP, SP_EPS, ALU.mult, ALU.max)
                    for ch in range(NCHB):
                        s_bl6.wait(v, ch, per=2)
                        v.tensor_scalar(sclm[:], sclf[ch % 2][:], Q_STEP, SP_EPS, ALU.mult, ALU.max)
                        v.reciprocal(sclm[:], sclm[:])
                        s_rsc6.tag(v.tensor_scalar(rscb[ch % 2][:], sclm[:], LR, None, ALU.mult))
                        v.wait_ge(s_bmm6.h, ch + 1)
                        s_btt6.tag(v.tensor_tensor(bppd[0:32, TB * ch:TB * (ch + 1)],
                                                   bstd[ch % 2][:], ps_bcd[:], ALU.mult))
                    v.memset(bppd[32:33, :], -Q_ZERO)

                @block.tensor
                def _(t):
                    for ch in range(NCHB):
                        t.wait_ge(s_rsc6.h, ch + 1)
                        if ch >= 1:
                            t.wait_ge(s_btt6.h, ch)
                        s_bmm6.tag(t.matmul(ps_bcd[:], one1x32d[:], rscb[ch % 2][:],
                                            start=True, stop=True))

        ps_d = [st.enter_context(nc.psum_tensor(f"psd{i}", [128, TB], F32)) for i in range(2)]
        ps_t = [st.enter_context(nc.psum_tensor(f"pst{i}", [128, TB], BF16)) for i in range(2)]
        ps_ud = st.enter_context(nc.psum_tensor("psud", [33, TB], F32))

        NTOT = NTB * FC
        with nc.Block() as block:
            @block.gpsimd
            def _(g):
                g.wait_ge(s_pc.h, 16 * 86)
                n = 0
                for tc in range(NTB):
                    for oc in range(FC):
                        n += 1
                        if n > 2:
                            g.wait_ge(s_d.h, n - 2)
                        s_dsl.tag(n - 1, g.dma_start(dsb[n % 2][:], slab_ap(dbf_s, 128 * oc, 128, IC)))

            @block.sync
            def _(sy):
                n = 0
                for tc in range(NTB):
                    if tc >= 1:
                        sy.wait_ge(s_d.h, FC * tc)
                    for piece in range(4):
                        ic0 = piece * 22
                        icn = min(22, IC - ic0)
                        if icn <= 0:
                            continue
                        ap_in = bass.AP(h_s, 128 * ic0 * S_LEN + TB * tc,
                                        [[S_LEN, 128], [128 * S_LEN, icn], [1, TB]])
                        s_hld[piece].tag(tc, sy.dma_start(sb3(hres, TB * ic0, (TB, icn), (1, TB)), ap_in))
                    for oc in range(FC):
                        n += 1
                        if n > 2:
                            sy.wait_ge(s_mm6.h, n - 2)
                        s_sld.tag(n - 1, sy.dma_start(st_sb[n % 2][:],
                                                      sT_s.ap()[128 * oc:128 * (oc + 1), TB * tc:TB * (tc + 1)]))
                        if n > 2:
                            sy.wait_ge(s_o6.h, n - 2)
                        s_xld.tag(n - 1, sy.dma_start(x_sb6[n % 2][:],
                                                      bass.AP(x_in, TB * tc * HID + 128 * oc,
                                                              [[HID, 128], [128 * HID, 4], [1, 128]])))
                        if n >= 2:
                            sy.wait_ge(s_o6.h, n - 1)
                            tcp, ocp = divmod(n - 2, FC)
                            s_ost.tag(n - 2, sy.dma_start(bass.AP(out, TB * tcp * HID + 128 * ocp,
                                                                  [[HID, 128], [128 * HID, 4], [1, 128]]),
                                                          o_sb[(n - 1) % 2][:]))
                sy.wait_ge(s_o6.h, NTOT)
                tcp, ocp = divmod(NTOT - 1, FC)
                s_ost.tag(NTOT - 1, sy.dma_start(bass.AP(out, TB * tcp * HID + 128 * ocp,
                                                         [[HID, 128], [128 * HID, 4], [1, 128]]),
                                                 o_sb[NTOT % 2][:]))
                s_ost.wait_all(sy)

            @block.tensor
            def _(t):
                n = 0
                for tc in range(NTB):
                    if tc >= 1:
                        t.wait_ge(s_udev.h, tc)
                    for ic in range(IC):
                        if ic % 22 == 0:
                            s_hld[ic // 22].wait(t, tc)
                        ins = t.matmul(ps_ud[:], aaugd[:, 33 * ic:33 * (ic + 1)],
                                       hres[:, TB * ic:TB * (ic + 1)],
                                       start=(ic == 0), stop=(ic == IC - 1))
                    s_ud.tag(ins)
                    for oc in range(FC):
                        n += 1
                        s_dsl.wait(t, n - 1)
                        if n > 2:
                            t.wait_ge(s_g.h, n - 2)
                        if oc == 0:
                            t.wait_ge(s_udev.h, tc + 1)
                        for ic in range(IC):
                            t.matmul(ps_d[n % 2][:], dsb[n % 2][:, 128 * ic:128 * (ic + 1)],
                                     hres[:, TB * ic:TB * (ic + 1)],
                                     start=(ic == 0), stop=False)
                        s_d.tag(t.matmul(ps_d[n % 2][:], bppd[:, 128 * oc:128 * (oc + 1)],
                                         ud_sb[:], start=False, stop=True))
                        if n >= 2:
                            t.wait_ge(s_mm6.h, n - 1)
                            if n > 3:
                                t.wait_ge(s_o6.h, n - 3)
                            for sblk in range(4):
                                ins = t.transpose(ps_t[(n - 1) % 2][:, 128 * sblk:128 * (sblk + 1)],
                                                  m_sb[(n - 1) % 2][:, 128 * sblk:128 * (sblk + 1)],
                                                  ident[:])
                            s_t6.tag(ins)
                t.wait_ge(s_mm6.h, NTOT)
                t.wait_ge(s_o6.h, NTOT - 2)
                for sblk in range(4):
                    ins = t.transpose(ps_t[NTOT % 2][:, 128 * sblk:128 * (sblk + 1)],
                                      m_sb[NTOT % 2][:, 128 * sblk:128 * (sblk + 1)], ident[:])
                s_t6.tag(ins)

            @block.scalar
            def _(sc):
                n = 0
                for tc in range(NTB):
                    sc.wait_ge(s_ud.h, tc + 1)
                    s_udev.tag(sc.copy(ud_sb[:], ps_ud[:]))
                    for oc in range(FC):
                        n += 1
                        sc.wait_ge(s_d.h, n)
                        if n > 2:
                            sc.wait_ge(s_mm6.h, n - 2)
                        s_g.tag(sc.activation(g_sb[n % 2][:], ps_d[n % 2][:], AF.Silu,
                                              scale=spcd[:, oc:oc + 1]))

            @block.vector
            def _(v):
                n = 0
                for tc in range(NTB):
                    for oc in range(FC):
                        n += 1
                        v.wait_ge(s_g.h, n)
                        s_sld.wait(v, n - 1)
                        if n > 2:
                            v.wait_ge(s_t6.h, n - 2)
                        s_mm6.tag(v.tensor_tensor(m_sb[n % 2][:], g_sb[n % 2][:], st_sb[n % 2][:], ALU.mult))
                        if n >= 2:
                            v.wait_ge(s_t6.h, n - 1)
                            s_xld.wait(v, n - 2)
                            if n > 3:
                                s_ost.wait(v, n - 4)
                            s_o6.tag(v.tensor_tensor(o_sb[(n - 1) % 2][:], ps_t[(n - 1) % 2][:],
                                                     x_sb6[(n - 1) % 2][:], ALU.add))
                v.wait_ge(s_t6.h, NTOT)
                s_xld.wait(v, NTOT - 1)
                s_ost.wait(v, NTOT - 3)
                s_o6.tag(v.tensor_tensor(o_sb[NTOT % 2][:], ps_t[NTOT % 2][:], x_sb6[NTOT % 2][:], ALU.add))


def stage_inputs(inputs):
    a = {k: np.asarray(v) for k, v in inputs.items()}
    common = {
        "wqT": np.ascontiguousarray(a["L0_attn_Wq"].T),
        "wkT": np.ascontiguousarray(a["L0_attn_Wk"].T),
        "wvT": np.ascontiguousarray(a["L0_attn_Wv"].T),
        "woT": np.ascontiguousarray(a["L0_attn_Wo"].T),
        "qgT": np.ascontiguousarray(a["L0_ffn_gate_q"].T),
        "quT": np.ascontiguousarray(a["L0_ffn_up_q"].T),
        "qdT": np.ascontiguousarray(a["L0_ffn_down_q"].T),
        "gsc_pc": np.ascontiguousarray(a["L0_ffn_gate_scale"].reshape(IC, 128).T),
        "usc_pc": np.ascontiguousarray(a["L0_ffn_up_scale"].reshape(IC, 128).T),
        "dsc_pc": np.ascontiguousarray(a["L0_ffn_down_scale"].reshape(FC, 128).T),
        "gsc_row": np.ascontiguousarray(a["L0_ffn_gate_scale"].reshape(1, IMED)),
        "usc_row": np.ascontiguousarray(a["L0_ffn_up_scale"].reshape(1, IMED)),
        "dsc_row": np.ascontiguousarray(a["L0_ffn_down_scale"].reshape(1, HID)),
        "gA": np.ascontiguousarray(a["L0_ffn_gate_A"]),
        "uA": np.ascontiguousarray(a["L0_ffn_up_A"]),
        "dA": np.ascontiguousarray(a["L0_ffn_down_A"]),
        "gB": np.ascontiguousarray(a["L0_ffn_gate_B"]),
        "uB": np.ascontiguousarray(a["L0_ffn_up_B"]),
        "dB": np.ascontiguousarray(a["L0_ffn_down_B"]),
    }
    x = a["L0_attn_hidden_states"]
    return [dict(common, x=np.ascontiguousarray(x[c])) for c in range(8)]


def kernel(**inputs):
    in_maps = stage_inputs(inputs)
    nc = build_program()
    res = run_bass_kernel_spmd(nc, in_maps, core_ids=list(range(8)))
    return np.stack([r["out"] for r in res.results]).astype(np.float32)


# version 05
# speedup vs baseline: 1.1405x; 1.1205x over previous
import numpy as np
import ml_dtypes
from contextlib import ExitStack

import concourse.bass as bass
import concourse.mybir as mybir
from concourse.bass_utils import run_bass_kernel_spmd

AF = mybir.ActivationFunctionType
ALU = mybir.AluOpType
F32 = mybir.dt.float32
BF16 = mybir.dt.bfloat16
I32 = mybir.dt.int32
AX = mybir.AxisListType.X

S_LEN = 2048
HID = 4096
HKV = 1024
IMED = 11008
NTT = 16
NTB = 4
TB = 512
FC = HID // 128
IC = IMED // 128
NH, NKV, DH = 32, 8, 128
EXP_SCALE = 1.0 / float(np.sqrt(128.0))
EXP_BIAS = -15.0
LN_EPS = 1e-5
SP_EPS = 1e-12
Q_ZERO, Q_STEP, LR = 7.5, 0.005, 0.02

ALL_PHASES = ("ln", "qkv", "gateup", "attn", "wo", "down")


def _host_consts():
    inv_freq = 1.0 / (10000.0 ** (np.arange(0, DH, 2, dtype=np.float32) / DH))
    t = np.arange(S_LEN, dtype=np.float32)
    freqs = np.outer(t, inv_freq)
    emb = np.concatenate([freqs, freqs], axis=1)
    cos_fm = np.ascontiguousarray(np.cos(emb).T.astype(np.float32))
    sin_fm = np.sin(emb).T.astype(np.float32)
    sinS_fm = np.ascontiguousarray(sin_fm * np.where(np.arange(DH)[:, None] < 64, -1.0, 1.0).astype(np.float32))
    ident = np.eye(128, dtype=ml_dtypes.bfloat16)
    prot = np.zeros((128, 128), dtype=ml_dtypes.bfloat16)
    prot[(np.arange(128) + 64) % 128, np.arange(128)] = 1
    masks = np.zeros((128, 4 * TB), dtype=ml_dtypes.bfloat16)
    jj = np.arange(128)[:, None]
    qq = np.arange(TB)[None, :]
    for r in range(4):
        masks[:, r * TB:(r + 1) * TB] = ((128 * r + jj) <= qq).astype(ml_dtypes.bfloat16)
    return ident, prot, cos_fm, sinS_fm, masks


def build_program(phases=ALL_PHASES, dbg=()):
    nc = bass.Bass("TRN2", target_bir_lowering=False)

    x_in = nc.dram_tensor("x", [S_LEN, HID], F32, kind="ExternalInput")
    wqT = nc.dram_tensor("wqT", [HID, HID], F32, kind="ExternalInput")
    wkT = nc.dram_tensor("wkT", [HID, HKV], F32, kind="ExternalInput")
    wvT = nc.dram_tensor("wvT", [HID, HKV], F32, kind="ExternalInput")
    woT = nc.dram_tensor("woT", [HID, HID], F32, kind="ExternalInput")
    qgT = nc.dram_tensor("qgT", [HID, IMED], I32, kind="ExternalInput")
    quT = nc.dram_tensor("quT", [HID, IMED], I32, kind="ExternalInput")
    qdT = nc.dram_tensor("qdT", [IMED, HID], I32, kind="ExternalInput")
    gsc_pc = nc.dram_tensor("gsc_pc", [128, IC], F32, kind="ExternalInput")
    usc_pc = nc.dram_tensor("usc_pc", [128, IC], F32, kind="ExternalInput")
    dsc_pc = nc.dram_tensor("dsc_pc", [128, FC], F32, kind="ExternalInput")
    gsc_row = nc.dram_tensor("gsc_row", [1, IMED], F32, kind="ExternalInput")
    usc_row = nc.dram_tensor("usc_row", [1, IMED], F32, kind="ExternalInput")
    dsc_row = nc.dram_tensor("dsc_row", [1, HID], F32, kind="ExternalInput")
    gA = nc.dram_tensor("gA", [HID, 32], F32, kind="ExternalInput")
    uA = nc.dram_tensor("uA", [HID, 32], F32, kind="ExternalInput")
    dA = nc.dram_tensor("dA", [IMED, 32], F32, kind="ExternalInput")
    gB = nc.dram_tensor("gB", [32, IMED], F32, kind="ExternalInput")
    uB = nc.dram_tensor("uB", [32, IMED], F32, kind="ExternalInput")
    dB = nc.dram_tensor("dB", [32, HID], F32, kind="ExternalInput")
    out = nc.dram_tensor("out", [S_LEN, HID], F32, kind="ExternalOutput")

    qT_s = nc.dram_tensor("qT_s", [HID, S_LEN], BF16)
    kT_s = nc.dram_tensor("kT_s", [HKV, S_LEN], BF16)
    vT_s = nc.dram_tensor("vT_s", [HKV, S_LEN], BF16)
    attn_s = nc.dram_tensor("attn_s", [HID, S_LEN], BF16)
    sT_s = nc.dram_tensor("sT_s", [HID, S_LEN], BF16)
    h_s = nc.dram_tensor("h_s", [IMED, S_LEN], BF16)
    dbf_s = nc.dram_tensor("dbf_s", [IMED, HID], BF16)

    ident_np, prot_np, cos_np, sinS_np, masks_np = _host_consts()
    ident_d = nc.inline_tensor(ident_np, "ident_c")
    prot_d = nc.inline_tensor(prot_np, "prot_c")
    cos_d = nc.inline_tensor(cos_np, "cos_c")
    sinS_d = nc.inline_tensor(sinS_np, "sinS_c")
    masks_d = nc.inline_tensor(masks_np, "masks_c")

    dbg_outs = {}
    for name, shape in (("bppg", [33, IMED]), ("bppu", [33, IMED]),
                        ("ugg", [33, S_LEN]), ("ugu", [33, S_LEN]),
                        ("xnT", [128, FC * S_LEN]), ("qT", [HID, S_LEN]),
                        ("kT", [HKV, S_LEN]), ("vT", [HKV, S_LEN]),
                        ("attn", [HID, S_LEN]), ("sT", [HID, S_LEN]),
                        ("h", [IMED, S_LEN]), ("dbf", [IMED, HID])):
        if name in dbg:
            dbg_outs[name] = nc.dram_tensor("dbg_" + name, shape, F32, kind="ExternalOutput")

    def slab_ap(dram, col0, ncols, nchunks):
        w = dram.shape[1]
        return bass.AP(dram, col0, [[w, 128], [128 * w, nchunks], [1, ncols]])

    def sb3(sbuf, off, d1, d2):
        row = int(np.prod(sbuf.shape[1:]))
        return bass.AP(sbuf, off, [[row, 128], list(d1), list(d2)])

    def dump_dram(scr, dram_out, sem):
        with nc.Block() as block:
            @block.gpsimd
            def _(g):
                g.dma_start(dram_out.ap(), scr.ap()).then_inc(sem.h, 16)
                g.wait_ge(sem.h, 16)

    stack_top = ExitStack()

    class Sem:
        __slots__ = ("h", "n")
        def __init__(self, name):
            self.h = stack_top.enter_context(nc.semaphore(name))
            self.n = 0
        def tag(self, instr, k=1):
            instr.then_inc(self.h, k)
            self.n += k
            return self.n

    class DSem:
        __slots__ = ("s",)
        def __init__(self, name):
            self.s = [Sem(name + "a"), Sem(name + "b")]
        def tag(self, i, instr):
            self.s[i % 2].tag(instr, 16)
        def wait(self, eng, i, per=1):
            eng.wait_ge(self.s[i % 2].h, 16 * per * (i // 2 + 1))
        def wait_all(self, eng):
            for sh in self.s:
                if sh.n:
                    eng.wait_ge(sh.h, sh.n)

    with stack_top:
        ident = stack_top.enter_context(nc.sbuf_tensor("ident", [128, 128], BF16))
        prot = stack_top.enter_context(nc.sbuf_tensor("prot", [128, 128], BF16))
        s_const = Sem("s_const")
        with nc.Block() as block:
            @block.sync
            def _(sy):
                s_const.tag(sy.dma_start(ident[:], ident_d.ap()), 16)
                s_const.tag(sy.dma_start(prot[:], prot_d.ap()), 16)
                sy.wait_ge(s_const.h, 32)

        s_pc = Sem("s_pc")
        precast_left = [86 if ("down" in phases) else 0]

        def emit_precast(g, k):
            for _ in range(k):
                if precast_left[0] == 0:
                    return
                ic = 86 - precast_left[0]
                s_pc.tag(g.dma_start(dbf_s.ap()[128 * ic:128 * (ic + 1), :],
                                     qdT.ap()[128 * ic:128 * (ic + 1), :]), 16)
                precast_left[0] -= 1

        with ExitStack() as stack_a:
            if "ln" in phases:
                xnT_res = stack_a.enter_context(nc.sbuf_tensor("xnT_res", [128, FC * S_LEN], BF16))
                _phase_ln(nc, Sem, DSem, x_in, xnT_res, ident)
                if "xnT" in dbg_outs:
                    sem = Sem("s_dbg_xnT")
                    with nc.Block() as block:
                        @block.gpsimd
                        def _(g):
                            g.dma_start(dbg_outs["xnT"].ap(), xnT_res[:]).then_inc(sem.h, 16)
                            g.wait_ge(sem.h, 16)

            if "qkv" in phases:
                _phase_qkv(nc, Sem, DSem, xnT_res, prot, cos_d, sinS_d,
                           wqT, wkT, wvT, qT_s, kT_s, vT_s, slab_ap, emit_precast)
                for nm, scr in (("qT", qT_s), ("kT", kT_s), ("vT", vT_s)):
                    if nm in dbg_outs:
                        dump_dram(scr, dbg_outs[nm], Sem("s_dbg_" + nm))

            if "gateup" in phases:
                _phase_gateup(nc, Sem, DSem, xnT_res, qgT, quT, gsc_pc, usc_pc,
                              gsc_row, usc_row, gA, uA, gB, uB, h_s, slab_ap, sb3, emit_precast,
                              dbg_sbufs={k: v for k, v in dbg_outs.items()
                                         if k in ("bppg", "bppu", "ugg", "ugu")} or None)
                if "h" in dbg_outs:
                    dump_dram(h_s, dbg_outs["h"], Sem("s_dbg_h"))

        if "attn" in phases:
            _phase_attn(nc, Sem, DSem, qT_s, kT_s, vT_s, attn_s, masks_d)
            if "attn" in dbg_outs:
                dump_dram(attn_s, dbg_outs["attn"], Sem("s_dbg_attn"))

        if "wo" in phases:
            _phase_wo(nc, Sem, DSem, woT, attn_s, sT_s, slab_ap)
            if "sT" in dbg_outs:
                dump_dram(sT_s, dbg_outs["sT"], Sem("s_dbg_sT"))

        if "dbf" in dbg_outs and "down" in phases:
            semd = Sem("s_dbg_dbf")
            with nc.Block() as block:
                @block.gpsimd
                def _(g):
                    emit_precast(g, 86)
                    g.wait_ge(s_pc.h, s_pc.n)
                    g.dma_start(dbg_outs["dbf"].ap(), dbf_s.ap()).then_inc(semd.h, 16)
                    g.wait_ge(semd.h, 16)

        if "down" in phases:
            _phase_down(nc, Sem, DSem, x_in, out, h_s, sT_s, dbf_s,
                        dsc_pc, dsc_row, dA, dB, ident, slab_ap, sb3,
                        emit_precast, s_pc)

    return nc


def _phase_ln(nc, Sem, DSem, x_in, xnT_res, ident):
    with ExitStack() as st:
        x_sb = [st.enter_context(nc.sbuf_tensor(f"x_sb{i}", [128, HID], F32)) for i in range(2)]
        xn_sb = [st.enter_context(nc.sbuf_tensor(f"xn_sb{i}", [128, HID], BF16)) for i in range(2)]
        trash = [st.enter_context(nc.sbuf_tensor(f"trash{i}", [128, HID], BF16)) for i in range(2)]
        stat = st.enter_context(nc.sbuf_tensor("stat", [128, 16], F32))
        ptr = [st.enter_context(nc.psum_tensor(f"ptr{i}", [128, 512], BF16)) for i in range(2)]
        s_x = DSem("s1_x")
        s_sq, s_var, s_rstd, s_xn, s_tr, s_evac = (
            Sem("s1_sq"), Sem("s1_var"),
            Sem("s1_rstd"), Sem("s1_xn"), Sem("s1_tr"), Sem("s1_evac"))

        def col(c):
            return stat[:, c:c + 1]

        def sb3l(sbuf, off, d1, d2):
            rw = int(np.prod(sbuf.shape[1:]))
            return bass.AP(sbuf, off, [[rw, 128], list(d1), list(d2)])

        with nc.Block() as block:
            @block.sync
            def _(sy):
                for i in range(NTT):
                    if i >= 2:
                        sy.wait_ge(s_xn.h, i - 1)
                    s_x.tag(i, sy.dma_start(x_sb[i % 2][:], x_in.ap()[128 * i:128 * (i + 1), :]))

            @block.scalar
            def _(sc):
                def rstd_pair(j):
                    b = j % 2
                    sc.wait_ge(s_var.h, j + 1)
                    if j >= 2:
                        sc.wait_ge(s_xn.h, j - 1)
                    sc.activation(col(14), col(10 + b), AF.Ln)
                    sc.activation(col(12 + b), col(14), AF.Exp, scale=-0.5)
                    s_rstd.tag(sc.copy(trash[b][:, 0:512], trash[b][:, 512:1024]))

                def evacs(jj):
                    for g in range(8):
                        G = 8 * jj + g
                        sc.wait_ge(s_tr.h, G + 1)
                        s_evac.tag(sc.copy(sb3l(xnT_res, (4 * g) * S_LEN + 128 * jj, (S_LEN, 4), (1, 128)),
                                           sb3l(ptr[G % 2], 0, (128, 4), (1, 128))))

                for i in range(NTT):
                    s_x.wait(sc, i)
                    if i >= 2:
                        sc.wait_ge(s_var.h, i - 1)
                        sc.wait_ge(s_rstd.h, i - 1)
                    s_sq.tag(sc.activation(trash[i % 2][:], x_sb[i % 2][:], AF.Square))
                    if i >= 1:
                        rstd_pair(i - 1)
                    if i >= 2:
                        evacs(i - 2)
                rstd_pair(NTT - 1)
                evacs(NTT - 2)
                evacs(NTT - 1)

            @block.vector
            def _(v):
                def stats_a(j):
                    b = j % 2
                    v.tensor_scalar(col(4 + b), col(0 + b), 1.0 / HID, None, ALU.mult)
                    v.tensor_scalar(col(6 + b), col(0 + b), col(0 + b), 1.0 / (HID * HID),
                                    ALU.mult, ALU.mult)
                    v.tensor_scalar(col(8 + b), col(2 + b), 1.0 / HID, LN_EPS, ALU.mult, ALU.add)

                def var_of(j):
                    b = j % 2
                    v.tensor_tensor(col(10 + b), col(8 + b), col(6 + b), ALU.subtract)

                def xn_of(j):
                    b = j % 2
                    v.wait_ge(s_rstd.h, j + 1)
                    if j >= 2:
                        v.wait_ge(s_tr.h, 8 * (j - 1))
                    s_xn.tag(v.tensor_scalar(xn_sb[b][:], x_sb[j % 2][:],
                                             col(4 + b), col(12 + b),
                                             ALU.subtract, ALU.mult))

                for i in range(NTT):
                    if i >= 1:
                        stats_a(i - 1)
                    s_x.wait(v, i)
                    v.reduce_sum(col(0 + i % 2), x_sb[i % 2][:], axis=AX)
                    if i >= 1:
                        var_of(i - 1)
                    v.wait_ge(s_sq.h, i + 1)
                    ins = v.reduce_sum(col(2 + i % 2), trash[i % 2][:], axis=AX)
                    if i >= 1:
                        s_var.tag(ins)
                    if i >= 1:
                        xn_of(i - 1)
                stats_a(NTT - 1)
                v.reduce_sum(col(15), x_sb[(NTT - 1) % 2][:], axis=AX)
                var_of(NTT - 1)
                s_var.tag(v.reduce_sum(col(15), x_sb[(NTT - 1) % 2][:], axis=AX))
                xn_of(NTT - 1)

            @block.tensor
            def _(t):
                for j in range(NTT):
                    t.wait_ge(s_xn.h, j + 1)
                    for g in range(8):
                        G = 8 * j + g
                        if G >= 2:
                            t.wait_ge(s_evac.h, G - 1)
                        for c4 in range(4):
                            c = 4 * g + c4
                            ins = t.transpose(ptr[G % 2][:, 128 * c4:128 * (c4 + 1)],
                                              xn_sb[j % 2][:, 128 * c:128 * (c + 1)], ident[:])
                        s_tr.tag(ins)


def _phase_qkv(nc, Sem, DSem, xnT_res, prot, cos_d, sinS_d,
               wqT, wkT, wvT, qT_s, kT_s, vT_s, slab_ap, emit_precast):
    with ExitStack() as st:
        cosf = st.enter_context(nc.sbuf_tensor("cosf", [128, S_LEN], F32))
        sinf = st.enter_context(nc.sbuf_tensor("sinf", [128, S_LEN], F32))
        wsl = [st.enter_context(nc.sbuf_tensor(f"wsl{i}", [128, FC * 128], BF16)) for i in range(2)]
        raw_bf = [st.enter_context(nc.sbuf_tensor(f"raw{i}", [128, S_LEN], BF16)) for i in range(2)]
        rope_sb = [st.enter_context(nc.sbuf_tensor(f"rope{i}", [128, S_LEN], BF16)) for i in range(2)]
        tmp1 = st.enter_context(nc.sbuf_tensor("tmp1", [128, TB], F32))
        tmp2 = st.enter_context(nc.sbuf_tensor("tmp2", [128, TB], F32))
        psp = [st.enter_context(nc.psum_tensor(f"psp{i}", [128, 4 * TB], F32)) for i in range(2)]
        s_ct = Sem("s2_ct")
        s_wld = DSem("s2_wld")
        s_proj = Sem("s2_proj")
        s_raw = Sem("s2_raw")
        s_rot = Sem("s2_rot")
        s_rope = Sem("s2_rope")
        s_store = DSem("s2_store")

        chunks = ([("q", dc) for dc in range(FC)] + [("k", dc) for dc in range(8)]
                  + [("v", dc) for dc in range(8)])
        NCH = len(chunks)
        wsrc = {"q": wqT, "k": wkT, "v": wvT}
        dsts = {"q": qT_s, "k": kT_s, "v": vT_s}

        with nc.Block() as block:
            @block.sync
            def _(sy):
                s_ct.tag(sy.dma_start(cosf[:], cos_d.ap()), 16)
                s_ct.tag(sy.dma_start(sinf[:], sinS_d.ap()), 16)
                for j, (kind, dc) in enumerate(chunks):
                    if kind == "v":
                        sy.wait_ge(s_raw.h, 4 * (j + 1))
                        src = raw_bf[j % 2]
                    else:
                        sy.wait_ge(s_rope.h, 4 * (j + 1))
                        src = rope_sb[j % 2]
                    s_store.tag(j, sy.dma_start(dsts[kind].ap()[128 * dc:128 * (dc + 1), :], src[:]))
                s_store.wait_all(sy)

            @block.gpsimd
            def _(g):
                for j, (kind, dc) in enumerate(chunks):
                    if j >= 2:
                        g.wait_ge(s_proj.h, j - 1)
                    s_wld.tag(j, g.dma_start(wsl[j % 2][:], slab_ap(wsrc[kind], 128 * dc, 128, FC)))
                    emit_precast(g, 2)

            @block.tensor
            def _(t):
                def emit_rot(i):
                    t.wait_ge(s_raw.h, 4 * (i + 1))
                    for tb in range(NTB):
                        s_rot.tag(t.matmul(psp[i % 2][:, TB * tb:TB * (tb + 1)], prot[:],
                                           raw_bf[i % 2][:, TB * tb:TB * (tb + 1)],
                                           start=True, stop=True))

                for j, (kind, dc) in enumerate(chunks):
                    s_wld.wait(t, j)
                    if j >= 2:
                        if chunks[j - 2][0] == "v":
                            t.wait_ge(s_raw.h, 4 * (j - 1))
                        else:
                            t.wait_ge(s_rope.h, 4 * (j - 1))
                    for f in range(FC):
                        for tb in range(NTB):
                            ins = t.matmul(psp[j % 2][:, TB * tb:TB * (tb + 1)],
                                           wsl[j % 2][:, 128 * f:128 * (f + 1)],
                                           xnT_res[:, S_LEN * f + TB * tb:S_LEN * f + TB * (tb + 1)],
                                           start=(f == 0), stop=(f == FC - 1))
                    s_proj.tag(ins)
                    if j >= 1 and chunks[j - 1][0] != "v":
                        emit_rot(j - 1)
                if chunks[NCH - 1][0] != "v":
                    emit_rot(NCH - 1)

            @block.scalar
            def _(sc):
                for j, (kind, dc) in enumerate(chunks):
                    sc.wait_ge(s_proj.h, j + 1)
                    if j >= 2:
                        s_store.wait(sc, j - 2)
                        if chunks[j - 2][0] != "v":
                            sc.wait_ge(s_rope.h, 4 * (j - 1))
                    for tb in range(NTB):
                        s_raw.tag(sc.copy(raw_bf[j % 2][:, TB * tb:TB * (tb + 1)],
                                          psp[j % 2][:, TB * tb:TB * (tb + 1)]))

            @block.vector
            def _(v):
                v.wait_ge(s_ct.h, 32)
                for j, (kind, dc) in enumerate(chunks):
                    if kind == "v":
                        continue
                    for tb in range(NTB):
                        v.wait_ge(s_raw.h, 4 * j + tb + 1)
                        if tb == 0 and j >= 2:
                            s_store.wait(v, j - 2)
                        v.tensor_tensor(tmp1[:], raw_bf[j % 2][:, TB * tb:TB * (tb + 1)],
                                        cosf[:, TB * tb:TB * (tb + 1)], ALU.mult)
                        v.wait_ge(s_rot.h, 4 * j + tb + 1)
                        v.tensor_tensor(tmp2[:], psp[j % 2][:, TB * tb:TB * (tb + 1)],
                                        sinf[:, TB * tb:TB * (tb + 1)], ALU.mult)
                        s_rope.tag(v.tensor_tensor(rope_sb[j % 2][:, TB * tb:TB * (tb + 1)],
                                                   tmp1[:], tmp2[:], ALU.add))


def _phase_gateup(nc, Sem, DSem, xnT_res, qgT, quT, gsc_pc, usc_pc,
                  gsc_row, usc_row, gA, uA, gB, uB, h_s, slab_ap, sb3, emit_precast,
                  dbg_sbufs=None):
    with ExitStack() as st:
        bpp = [st.enter_context(nc.sbuf_tensor(f"bpp{i}", [33, IMED], BF16)) for i in range(2)]
        ug_sb = [st.enter_context(nc.sbuf_tensor(f"ug{i}", [33, S_LEN], BF16)) for i in range(2)]
        spc = [st.enter_context(nc.sbuf_tensor(f"spc{i}", [128, IC], F32)) for i in range(2)]
        sg_sb = [st.enter_context(nc.sbuf_tensor(f"sg{i}", [128, TB], BF16)) for i in range(4)]
        uu_sb = [st.enter_context(nc.sbuf_tensor(f"uu{i}", [128, TB], BF16)) for i in range(4)]
        h_sb = [st.enter_context(nc.sbuf_tensor(f"hsb{i}", [128, TB], BF16)) for i in range(2)]

        s_pl = Sem("s5_pl")
        s_pg = Sem("s5_pg")
        s_umm = Sem("s5_umm")
        s_uev = Sem("s5_uev")
        s_bld = DSem("s5_bld")
        s_rsc = Sem("s5_rsc")
        s_bmm = Sem("s5_bmm")
        s_btt = Sem("s5_btt")
        s_wg = Sem("s5_wg")
        s_wu = Sem("s5_wu")
        s_gp = Sem("s5_gp")
        s_up = Sem("s5_up")
        s_sil = Sem("s5_sil")
        s_cpy = Sem("s5_cpy")
        s_h = Sem("s5_h")
        s_hd = DSem("s5_hd")

        with ExitStack() as pre:
            aaug = [pre.enter_context(nc.sbuf_tensor(f"aaug{i}", [128, FC * 33], BF16)) for i in range(2)]
            sclf = [pre.enter_context(nc.sbuf_tensor(f"sclf{i}", [1, TB], F32)) for i in range(2)]
            sclm = pre.enter_context(nc.sbuf_tensor("sclm", [1, TB], F32))
            rscb = [pre.enter_context(nc.sbuf_tensor(f"rscb{i}", [1, TB], BF16)) for i in range(2)]
            bst = [pre.enter_context(nc.sbuf_tensor(f"bst{i}", [32, TB], F32)) for i in range(2)]
            one1x32 = pre.enter_context(nc.sbuf_tensor("one1x32", [1, 32], BF16))
            ps_a = pre.enter_context(nc.psum_tensor("ps_a", [33, TB], F32))
            ps_bc = pre.enter_context(nc.psum_tensor("ps_bc", [32, TB], F32))

            NCHB = (IMED + TB - 1) // TB
            chw = [min(TB, IMED - TB * c) for c in range(NCHB)]
            s_ms = Sem("s5_ms")
            with nc.Block() as block:
                @block.sync
                def _(sy):
                    s_pl.tag(sy.dma_start(spc[0][:], gsc_pc.ap()), 16)
                    s_pl.tag(sy.dma_start(spc[1][:], usc_pc.ap()), 16)
                    for gg, scrow in ((0, gsc_row), (1, usc_row)):
                        Bsrc = gB if gg == 0 else uB
                        for ch in range(NCHB):
                            gi = gg * NCHB + ch
                            w = chw[ch]
                            if gi >= 2:
                                sy.wait_ge(s_btt.h, gi - 1)
                            s_bld.tag(gi, sy.dma_start(sclf[gi % 2][:, 0:w], scrow.ap()[:, TB * ch:TB * ch + w]))
                            s_bld.tag(gi, sy.dma_start(bst[gi % 2][:, 0:w], Bsrc.ap()[:, TB * ch:TB * ch + w]))

                @block.gpsimd
                def _(g):
                    s_pg.tag(g.dma_start(sb3(aaug[0], 0, (33, FC), (1, 32)),
                                         bass.AP(gA, 0, [[32, 128], [128 * 32, FC], [1, 32]])), 16)
                    s_pg.tag(g.dma_start(sb3(aaug[1], 0, (33, FC), (1, 32)),
                                         bass.AP(uA, 0, [[32, 128], [128 * 32, FC], [1, 32]])), 16)
                    emit_precast(g, 3)

                @block.vector
                def _(v):
                    v.memset(one1x32[:], 1.0)
                    v.wait_ge(s_pg.h, 32)
                    for gg in range(2):
                        v.memset(bass.AP(aaug[gg], 32, [[FC * 33, 128], [33, FC]]), 1.0)
                    s_ms.tag(v.memset(one1x32[:], 1.0))
                    v.wait_ge(s_pl.h, 32)
                    for gg in range(2):
                        v.tensor_scalar(spc[gg][:], spc[gg][:], Q_STEP, SP_EPS, ALU.mult, ALU.max)
                    for gg in range(2):
                        for ch in range(NCHB):
                            gi = gg * NCHB + ch
                            w = chw[ch]
                            s_bld.wait(v, gi, per=2)
                            v.tensor_scalar(sclm[:, 0:w], sclf[gi % 2][:, 0:w], Q_STEP, SP_EPS, ALU.mult, ALU.max)
                            v.reciprocal(sclm[:, 0:w], sclm[:, 0:w])
                            s_rsc.tag(v.tensor_scalar(rscb[gi % 2][:, 0:w], sclm[:, 0:w], LR, None, ALU.mult))
                            v.wait_ge(s_bmm.h, gi + 1)
                            s_btt.tag(v.tensor_tensor(bpp[gg][0:32, TB * ch:TB * ch + w],
                                                      bst[gi % 2][:, 0:w], ps_bc[:, 0:w], ALU.mult))
                        v.memset(bpp[gg][32:33, :], -Q_ZERO)

                @block.tensor
                def _(t):
                    t.wait_ge(s_pg.h, 32)
                    t.wait_ge(s_ms.h, 1)
                    for gg in range(2):
                        for tb in range(NTB):
                            gi = 4 * gg + tb
                            if gi >= 1:
                                t.wait_ge(s_uev.h, gi)
                            for f in range(FC):
                                ins = t.matmul(ps_a[:], aaug[gg][:, 33 * f:33 * (f + 1)],
                                               xnT_res[:, S_LEN * f + TB * tb:S_LEN * f + TB * (tb + 1)],
                                               start=(f == 0), stop=(f == FC - 1))
                            s_umm.tag(ins)
                    for gg in range(2):
                        for ch in range(NCHB):
                            gi = gg * NCHB + ch
                            w = chw[ch]
                            t.wait_ge(s_rsc.h, gi + 1)
                            if gi >= 1:
                                t.wait_ge(s_btt.h, gi)
                            s_bmm.tag(t.matmul(ps_bc[:, 0:w], one1x32[:], rscb[gi % 2][:, 0:w],
                                               start=True, stop=True))

                @block.scalar
                def _(sc):
                    for gi in range(8):
                        sc.wait_ge(s_umm.h, gi + 1)
                        gg, tb = gi // 4, gi % 4
                        s_uev.tag(sc.copy(ug_sb[gg][:, TB * tb:TB * (tb + 1)], ps_a[:]))

        gsl = st.enter_context(nc.sbuf_tensor("gsl", [128, FC * 128], BF16))
        usl = st.enter_context(nc.sbuf_tensor("usl", [128, FC * 128], BF16))
        ps_g = [stackless_psum(nc, st, f"psg{i}", [128, 2 * TB]) for i in range(2)]
        ps_u = [stackless_psum(nc, st, f"psu{i}", [128, 2 * TB]) for i in range(2)]

        def psbank(ps, tb):
            return ps[tb // 2][:, TB * (tb % 2):TB * (tb % 2 + 1)]

        with nc.Block() as block:
            @block.gpsimd
            def _(g):
                for oc in range(IC):
                    if oc >= 1:
                        g.wait_ge(s_gp.h, oc)
                    s_wg.tag(g.dma_start(gsl[:], slab_ap(qgT, 128 * oc, 128, FC)), 16)
                    emit_precast(g, 1)
                    if oc >= 1:
                        g.wait_ge(s_up.h, oc)
                    s_wu.tag(g.dma_start(usl[:], slab_ap(quT, 128 * oc, 128, FC)), 16)

            @block.tensor
            def _(t):
                for oc in range(IC):
                    t.wait_ge(s_wg.h, 16 * (oc + 1))
                    if oc >= 1:
                        t.wait_ge(s_sil.h, 4 * oc)
                    for f in range(FC):
                        for tb in range(NTB):
                            t.matmul(psbank(ps_g, tb),
                                     gsl[:, 128 * f:128 * (f + 1)],
                                     xnT_res[:, S_LEN * f + TB * tb:S_LEN * f + TB * (tb + 1)],
                                     start=(f == 0), stop=False)
                    for tb in range(NTB):
                        last = t.matmul(psbank(ps_g, tb),
                                        bpp[0][:, 128 * oc:128 * (oc + 1)],
                                        ug_sb[0][:, TB * tb:TB * (tb + 1)],
                                        start=False, stop=True)
                    s_gp.tag(last)
                    t.wait_ge(s_wu.h, 16 * (oc + 1))
                    if oc >= 1:
                        t.wait_ge(s_cpy.h, 4 * oc)
                    for f in range(FC):
                        for tb in range(NTB):
                            t.matmul(psbank(ps_u, tb),
                                     usl[:, 128 * f:128 * (f + 1)],
                                     xnT_res[:, S_LEN * f + TB * tb:S_LEN * f + TB * (tb + 1)],
                                     start=(f == 0), stop=False)
                    for tb in range(NTB):
                        last = t.matmul(psbank(ps_u, tb),
                                        bpp[1][:, 128 * oc:128 * (oc + 1)],
                                        ug_sb[1][:, TB * tb:TB * (tb + 1)],
                                        start=False, stop=True)
                    s_up.tag(last)

            @block.scalar
            def _(sc):
                cnt = 0
                for oc in range(IC):
                    sc.wait_ge(s_gp.h, oc + 1)
                    for tb in range(NTB):
                        cnt += 1
                        if cnt > 4:
                            sc.wait_ge(s_h.h, cnt - 4)
                        s_sil.tag(sc.activation(sg_sb[cnt % 4][:], psbank(ps_g, tb),
                                                AF.Silu, scale=spc[0][:, oc:oc + 1]))
                    sc.wait_ge(s_up.h, oc + 1)
                    for tb in range(NTB):
                        cnt2 = 4 * oc + tb + 1
                        if cnt2 > 4:
                            sc.wait_ge(s_h.h, cnt2 - 4)
                        s_cpy.tag(sc.activation(uu_sb[cnt2 % 4][:], psbank(ps_u, tb),
                                                AF.Copy, scale=spc[1][:, oc:oc + 1]))

            @block.vector
            def _(v):
                cnt = 0
                for oc in range(IC):
                    for tb in range(NTB):
                        cnt += 1
                        v.wait_ge(s_cpy.h, cnt)
                        if cnt > 2:
                            s_hd.wait(v, cnt - 3)
                        s_h.tag(v.tensor_tensor(h_sb[cnt % 2][:], sg_sb[cnt % 4][:], uu_sb[cnt % 4][:], ALU.mult))

            @block.sync
            def _(sy):
                cnt = 0
                for oc in range(IC):
                    for tb in range(NTB):
                        cnt += 1
                        sy.wait_ge(s_h.h, cnt)
                        s_hd.tag(cnt - 1, sy.dma_start(h_s.ap()[128 * oc:128 * (oc + 1), TB * tb:TB * (tb + 1)],
                                                       h_sb[cnt % 2][:]))
                s_hd.wait_all(sy)


def stackless_psum(nc, st, name, shape):
    return st.enter_context(nc.psum_tensor(name, shape, F32))


def _phase_attn(nc, Sem, DSem, qT_s, kT_s, vT_s, attn_s, masks_d):
    with ExitStack() as st:
        kres = st.enter_context(nc.sbuf_tensor("kres", [128, NKV * S_LEN], BF16))
        vres = st.enter_context(nc.sbuf_tensor("vres", [128, NTT * HKV], BF16))
        maskr = st.enter_context(nc.sbuf_tensor("maskr", [128, 4 * TB], BF16))
        qt_sb = [st.enter_context(nc.sbuf_tensor(f"qt{i}", [128, TB], BF16)) for i in range(4)]
        et_sb = [st.enter_context(nc.sbuf_tensor(f"et{i}", [128, TB], BF16)) for i in range(4)]
        etm_sb = [st.enter_context(nc.sbuf_tensor(f"etm{i}", [128, TB], BF16)) for i in range(4)]
        rz_f = [st.enter_context(nc.sbuf_tensor(f"rzf{i}", [1, TB], F32)) for i in range(2)]
        rz_bf = [st.enter_context(nc.sbuf_tensor(f"rzb{i}", [1, TB], BF16)) for i in range(2)]
        rzb_sb = [st.enter_context(nc.sbuf_tensor(f"rzbs{i}", [128, TB], BF16)) for i in range(2)]
        ao_sb = [st.enter_context(nc.sbuf_tensor(f"ao{i}", [128, TB], BF16)) for i in range(4)]
        zacc = [st.enter_context(nc.sbuf_tensor(f"zacc{i}", [128, TB], BF16)) for i in range(4)]
        ones_z = st.enter_context(nc.sbuf_tensor("ones_z", [128, 1], BF16))
        ones_r = st.enter_context(nc.sbuf_tensor("ones_r", [1, 128], BF16))
        ebias = st.enter_context(nc.sbuf_tensor("ebias", [128, 1], F32))
        ps_s = [st.enter_context(nc.psum_tensor(f"pss{i}", [128, TB], F32)) for i in range(4)]
        ps_av = [st.enter_context(nc.psum_tensor(f"psav{i}", [128, TB], F32)) for i in range(2)]
        ps_z = st.enter_context(nc.psum_tensor("psz", [128, TB], F32))
        ps_b = st.enter_context(nc.psum_tensor("psb", [128, TB], F32))

        s_res = Sem("s3_res")
        s_qld = DSem("s3_qld")
        s_s = Sem("s3_s")
        s_e = Sem("s3_e")
        s_m = Sem("s3_m")
        s_av = Sem("s3_av")
        s_rz = Sem("s3_rz")
        s_b = Sem("s3_b")
        s_rzb = Sem("s3_rzb")
        s_attn = Sem("s3_attn")
        s_zac = Sem("s3_zac")
        s_zmm = Sem("s3_zmm")
        s_ast = DSem("s3_ast")

        pairs = [(h, p) for h in range(NH) for p in range(2)]
        NPAIR = len(pairs)
        chunk_no = {}
        exp_waits = {}
        msk_cum = [0]
        pair_last = []
        zlast = {}
        c = 0
        m = 0
        for pidx, (h, p) in enumerate(pairs):
            Qlo, Qhi = 2 * p, 2 * p + 1
            nlo, nhi = 4 * (Qlo + 1), 4 * (Qhi + 1)
            for j in range(nhi):
                for mem, Q, nj in ((0, Qlo, nlo), (1, Qhi, nhi)):
                    if j >= nj:
                        continue
                    c += 1
                    chunk_no[(pidx, j, mem)] = c
                    if j >= 4 * Q:
                        m += 1
                        exp_waits[(pidx, j, mem)] = ("m", m)
                    else:
                        exp_waits[(pidx, j, mem)] = ("e", c)
                    msk_cum.append(m)
                    zlast[(pidx, mem)] = c
            pair_last.append(c)

        with nc.Block() as block:
            @block.vector
            def _(v):
                v.memset(ones_z[:], 1.0)
                v.memset(ones_r[:], 1.0)
                v.memset(ebias[:], EXP_BIAS)

        with nc.Block() as block:
            @block.sync
            def _(sy):
                for g in range(NKV):
                    s_res.tag(sy.dma_start(kres[:, S_LEN * g:S_LEN * (g + 1)],
                                           kT_s.ap()[128 * g:128 * (g + 1), :]), 16)
                s_res.tag(sy.dma_start(maskr[:], masks_d.ap()), 16)
                for jc in range(NTT):
                    for g in range(NKV):
                        s_res.tag(sy.dma_start_transpose(
                            vres[:, HKV * jc + 128 * g:HKV * jc + 128 * (g + 1)],
                            vT_s.ap()[128 * g:128 * (g + 1), 128 * jc:128 * (jc + 1)]), 16)
                for pidx, (h, p) in enumerate(pairs):
                    if pidx >= 2:
                        sy.wait_ge(s_s.h, pair_last[pidx - 2])
                    for mem in range(2):
                        Q = 2 * p + mem
                        s_qld.tag(2 * pidx + mem,
                                  sy.dma_start(qt_sb[(2 * pidx + mem) % 4][:],
                                               qT_s.ap()[128 * h:128 * (h + 1), TB * Q:TB * (Q + 1)]))
                    if pidx >= 1:
                        hp, pp = pairs[pidx - 1]
                        for mem in range(2):
                            k = 2 * (pidx - 1) + mem
                            sy.wait_ge(s_attn.h, k + 1)
                            Qm = 2 * pp + mem
                            s_ast.tag(k, sy.dma_start(
                                attn_s.ap()[128 * hp:128 * (hp + 1), TB * Qm:TB * (Qm + 1)],
                                ao_sb[k % 4][:]))
                hp, pp = pairs[NPAIR - 1]
                for mem in range(2):
                    k = 2 * (NPAIR - 1) + mem
                    sy.wait_ge(s_attn.h, k + 1)
                    Qm = 2 * pp + mem
                    s_ast.tag(k, sy.dma_start(
                        attn_s.ap()[128 * hp:128 * (hp + 1), TB * Qm:TB * (Qm + 1)],
                        ao_sb[k % 4][:]))
                s_ast.wait_all(sy)

            @block.vector
            def _(v):
                for pidx, (h, p) in enumerate(pairs):
                    Qlo, Qhi = 2 * p, 2 * p + 1
                    nlo, nhi = 4 * (Qlo + 1), 4 * (Qhi + 1)
                    for j in range(nhi):
                        for mem, Q, nj in ((0, Qlo, nlo), (1, Qhi, nhi)):
                            if j >= nj:
                                continue
                            cc = chunk_no[(pidx, j, mem)]
                            v.wait_ge(s_e.h, cc)
                            if j >= 4 * Q:
                                if cc > 4:
                                    v.wait_ge(s_av.h, cc - 4)
                                r = j - 4 * Q
                                s_m.tag(v.tensor_tensor(etm_sb[cc % 4][:], et_sb[cc % 4][:],
                                                        maskr[:, TB * r:TB * (r + 1)], ALU.mult))
                                esrc = etm_sb[cc % 4]
                            else:
                                esrc = et_sb[cc % 4]
                            zi = 2 * (pidx % 2) + mem
                            if j == 0:
                                if pidx >= 2:
                                    v.wait_ge(s_zmm.h, 2 * (pidx - 1) + mem - 1)
                                s_zac.tag(v.tensor_copy(zacc[zi][:], esrc[:]))
                            else:
                                s_zac.tag(v.tensor_tensor(zacc[zi][:], zacc[zi][:], esrc[:], ALU.add))
                    v.wait_ge(s_zmm.h, 2 * (pidx + 1))
                    for mem in range(2):
                        k = 2 * pidx + mem
                        v.reciprocal(rz_f[mem][:], ps_z[32 * mem:32 * mem + 1, :])
                        s_rz.tag(v.tensor_copy(rz_bf[mem][:], rz_f[mem][:]))
                        v.wait_ge(s_rzb.h, k + 1)
                        if pidx >= 2:
                            s_ast.wait(v, k - 4)
                        v.wait_ge(s_av.h, zlast[(pidx, mem)])
                        s_attn.tag(v.tensor_tensor(ao_sb[k % 4][:], ps_av[mem][:],
                                                   rzb_sb[mem][:], ALU.mult))

            @block.scalar
            def _(sc):
                for pidx, (h, p) in enumerate(pairs):
                    Qlo, Qhi = 2 * p, 2 * p + 1
                    nlo, nhi = 4 * (Qlo + 1), 4 * (Qhi + 1)
                    for j in range(nhi):
                        for mem, Q, nj in ((0, Qlo, nlo), (1, Qhi, nhi)):
                            if j >= nj:
                                continue
                            cc = chunk_no[(pidx, j, mem)]
                            sc.wait_ge(s_s.h, cc)
                            if cc > 4:
                                sc.wait_ge(s_av.h, cc - 4)
                                sc.wait_ge(s_zac.h, cc - 4)
                            s_e.tag(sc.activation(et_sb[cc % 4][:], ps_s[(cc - 1) % 4][:], AF.Exp,
                                                  scale=EXP_SCALE, bias=ebias[:, 0:1]))
                    for mem in range(2):
                        k = 2 * pidx + mem
                        sc.wait_ge(s_b.h, k + 1)
                        s_rzb.tag(sc.copy(rzb_sb[mem][:], ps_b[:]))

            @block.tensor
            def _(t):
                t.wait_ge(s_res.h, s_res.n)

                def emit_av(pidx, j, mem):
                    h, p = pairs[pidx]
                    g = h // 4
                    Q = 2 * p + mem
                    nj = 4 * (Q + 1)
                    if j == 0 and pidx >= 1:
                        t.wait_ge(s_attn.h, 2 * pidx)
                    cc = chunk_no[(pidx, j, mem)]
                    kind, val = exp_waits[(pidx, j, mem)]
                    t.wait_ge(s_m.h if kind == "m" else s_e.h, val)
                    esrc = etm_sb[cc % 4] if kind == "m" else et_sb[cc % 4]
                    s_av.tag(t.matmul(ps_av[mem][:],
                                      vres[:, HKV * j + 128 * g:HKV * j + 128 * (g + 1)],
                                      esrc[:], start=(j == 0), stop=(j == nj - 1)))

                def emit_epilogue_mm(pidx):
                    for mem in range(2):
                        k = 2 * pidx + mem
                        zi = 2 * (pidx % 2) + mem
                        t.wait_ge(s_zac.h, zlast[(pidx, mem)])
                        if pidx >= 1:
                            t.wait_ge(s_rzb.h, 2 * pidx + mem - 1)
                        s_zmm.tag(t.matmul(ps_z[32 * mem:32 * mem + 1, :], ones_z[:],
                                           zacc[zi][:], start=True, stop=True))

                def emit_bcast(pidx, mem):
                    k = 2 * pidx + mem
                    t.wait_ge(s_rz.h, k + 1)
                    if k >= 1:
                        t.wait_ge(s_rzb.h, k)
                    s_b.tag(t.matmul(ps_b[:], ones_r[:], rz_bf[mem][:],
                                     start=True, stop=True))

                pend = []
                for pidx, (h, p) in enumerate(pairs):
                    g = h // 4
                    Qlo, Qhi = 2 * p, 2 * p + 1
                    nlo, nhi = 4 * (Qlo + 1), 4 * (Qhi + 1)
                    s_qld.wait(t, 2 * pidx + 1)
                    for j in range(nhi):
                        for mem, nj in ((0, nlo), (1, nhi)):
                            if j >= nj:
                                continue
                            cc = chunk_no[(pidx, j, mem)]
                            if cc > 4:
                                t.wait_ge(s_e.h, cc - 4)
                            s_s.tag(t.matmul(ps_s[(cc - 1) % 4][:],
                                             kres[:, S_LEN * g + 128 * j:S_LEN * g + 128 * (j + 1)],
                                             qt_sb[(2 * pidx + mem) % 4][:], start=True, stop=True))
                        if pend:
                            emit_bcast(*pend.pop(0))
                        if j >= 2:
                            for mem, nj in ((0, nlo), (1, nhi)):
                                if j - 2 < nj:
                                    emit_av(pidx, j - 2, mem)
                    for jt in (nhi - 2, nhi - 1):
                        for mem, nj in ((0, nlo), (1, nhi)):
                            if jt < nj:
                                emit_av(pidx, jt, mem)
                    emit_epilogue_mm(pidx)
                    pend = [(pidx, 0), (pidx, 1)]
                for pe_ in pend:
                    emit_bcast(*pe_)
def _phase_wo(nc, Sem, DSem, woT, attn_s, sT_s, slab_ap):
    with ExitStack() as st:
        ares = st.enter_context(nc.sbuf_tensor("ares", [128, FC * S_LEN], BF16))
        wsl4 = [st.enter_context(nc.sbuf_tensor(f"wsl4{i}", [128, FC * 128], BF16)) for i in range(2)]
        s_sb4 = [st.enter_context(nc.sbuf_tensor(f"ssb4{i}", [128, S_LEN], BF16)) for i in range(2)]
        ps4 = [st.enter_context(nc.psum_tensor(f"ps4{i}", [128, 4 * TB], F32)) for i in range(2)]
        s_ares = Sem("s4_ares")
        s_wld4 = DSem("s4_wld")
        s_p4 = Sem("s4_p")
        s_ev4 = Sem("s4_ev")
        s_st4 = DSem("s4_st")

        with nc.Block() as block:
            @block.sync
            def _(sy):
                for f in range(FC):
                    s_ares.tag(sy.dma_start(ares[:, S_LEN * f:S_LEN * (f + 1)],
                                            attn_s.ap()[128 * f:128 * (f + 1), :]), 16)
                for oc in range(FC):
                    sy.wait_ge(s_ev4.h, 4 * (oc + 1))
                    s_st4.tag(oc, sy.dma_start(sT_s.ap()[128 * oc:128 * (oc + 1), :], s_sb4[oc % 2][:]))
                s_st4.wait_all(sy)

            @block.gpsimd
            def _(g):
                for oc in range(FC):
                    if oc >= 2:
                        g.wait_ge(s_p4.h, oc - 1)
                    s_wld4.tag(oc, g.dma_start(wsl4[oc % 2][:], slab_ap(woT, 128 * oc, 128, FC)))

            @block.tensor
            def _(t):
                for oc in range(FC):
                    s_wld4.wait(t, oc)
                    if oc == 0:
                        t.wait_ge(s_ares.h, 16 * FC)
                    if oc >= 2:
                        t.wait_ge(s_ev4.h, 4 * (oc - 1))
                    for f in range(FC):
                        for tb in range(NTB):
                            ins = t.matmul(ps4[oc % 2][:, TB * tb:TB * (tb + 1)],
                                           wsl4[oc % 2][:, 128 * f:128 * (f + 1)],
                                           ares[:, S_LEN * f + TB * tb:S_LEN * f + TB * (tb + 1)],
                                           start=(f == 0), stop=(f == FC - 1))
                    s_p4.tag(ins)

            @block.scalar
            def _(sc):
                for oc in range(FC):
                    sc.wait_ge(s_p4.h, oc + 1)
                    if oc >= 2:
                        s_st4.wait(sc, oc - 2)
                    for tb in range(NTB):
                        s_ev4.tag(sc.copy(s_sb4[oc % 2][:, TB * tb:TB * (tb + 1)],
                                          ps4[oc % 2][:, TB * tb:TB * (tb + 1)]))


def _phase_down(nc, Sem, DSem, x_in, out, h_s, sT_s, dbf_s,
                dsc_pc, dsc_row, dA, dB, ident, slab_ap, sb3, emit_precast, s_pc):
    with ExitStack() as st:
        hres = st.enter_context(nc.sbuf_tensor("hres", [128, IC * TB], BF16))
        dsb = [st.enter_context(nc.sbuf_tensor(f"dsb{i}", [128, IC * 128], BF16)) for i in range(2)]
        aaugd = st.enter_context(nc.sbuf_tensor("aaugd", [128, IC * 33], BF16))
        bppd = st.enter_context(nc.sbuf_tensor("bppd", [33, HID], BF16))
        ud_sb = st.enter_context(nc.sbuf_tensor("ud_sb", [33, TB], BF16))
        spcd = st.enter_context(nc.sbuf_tensor("spcd", [128, FC], F32))
        g_sb = [st.enter_context(nc.sbuf_tensor(f"g6{i}", [128, TB], BF16)) for i in range(2)]
        m_sb = [st.enter_context(nc.sbuf_tensor(f"m6{i}", [128, TB], BF16)) for i in range(2)]
        st_sb = [st.enter_context(nc.sbuf_tensor(f"st6{i}", [128, TB], BF16)) for i in range(2)]
        x_sb6 = [st.enter_context(nc.sbuf_tensor(f"x6{i}", [128, TB], F32)) for i in range(2)]
        o_sb = [st.enter_context(nc.sbuf_tensor(f"o6{i}", [128, TB], F32)) for i in range(2)]

        s_pl6 = Sem("s6_pl")
        s_bl6 = DSem("s6_bl")
        s_pg6 = Sem("s6_pg")
        s_rsc6 = Sem("s6_rsc")
        s_bmm6 = Sem("s6_bmm")
        s_btt6 = Sem("s6_btt")
        s_hld = [DSem(f"s6_hld{p}") for p in range(4)]
        s_dsl = DSem("s6_dsl")
        s_ud = Sem("s6_ud")
        s_udev = Sem("s6_udev")
        s_d = Sem("s6_d")
        s_g = Sem("s6_g")
        s_sld = DSem("s6_sld")
        s_xld = DSem("s6_xld")
        s_mm6 = Sem("s6_m")
        s_t6 = Sem("s6_t")
        s_o6 = Sem("s6_o")
        s_ost = DSem("s6_ost")

        with ExitStack() as pre:
            sclf = [pre.enter_context(nc.sbuf_tensor(f"sclf6{i}", [1, TB], F32)) for i in range(2)]
            sclm = pre.enter_context(nc.sbuf_tensor("sclm6", [1, TB], F32))
            rscb = [pre.enter_context(nc.sbuf_tensor(f"rscb6{i}", [1, TB], BF16)) for i in range(2)]
            bstd = [pre.enter_context(nc.sbuf_tensor(f"bstd{i}", [32, TB], F32)) for i in range(2)]
            one1x32d = pre.enter_context(nc.sbuf_tensor("one1x32d", [1, 32], BF16))
            ps_bcd = pre.enter_context(nc.psum_tensor("ps_bcd", [32, TB], F32))
            NCHB = HID // TB

            with nc.Block() as block:
                @block.sync
                def _(sy):
                    s_pl6.tag(sy.dma_start(spcd[:], dsc_pc.ap()), 16)
                    for ch in range(NCHB):
                        if ch >= 2:
                            sy.wait_ge(s_btt6.h, ch - 1)
                        s_bl6.tag(ch, sy.dma_start(sclf[ch % 2][:], dsc_row.ap()[:, TB * ch:TB * (ch + 1)]))
                        s_bl6.tag(ch, sy.dma_start(bstd[ch % 2][:], dB.ap()[:, TB * ch:TB * (ch + 1)]))

                @block.gpsimd
                def _(g):
                    emit_precast(g, 86)
                    s_pg6.tag(g.dma_start(sb3(aaugd, 0, (33, IC), (1, 32)),
                                          bass.AP(dA, 0, [[32, 128], [128 * 32, IC], [1, 32]])), 16)

                @block.vector
                def _(v):
                    v.memset(one1x32d[:], 1.0)
                    v.wait_ge(s_pg6.h, 16)
                    v.memset(bass.AP(aaugd, 32, [[IC * 33, 128], [33, IC]]), 1.0)
                    v.wait_ge(s_pl6.h, 16)
                    v.tensor_scalar(spcd[:], spcd[:], Q_STEP, SP_EPS, ALU.mult, ALU.max)
                    for ch in range(NCHB):
                        s_bl6.wait(v, ch, per=2)
                        v.tensor_scalar(sclm[:], sclf[ch % 2][:], Q_STEP, SP_EPS, ALU.mult, ALU.max)
                        v.reciprocal(sclm[:], sclm[:])
                        s_rsc6.tag(v.tensor_scalar(rscb[ch % 2][:], sclm[:], LR, None, ALU.mult))
                        v.wait_ge(s_bmm6.h, ch + 1)
                        s_btt6.tag(v.tensor_tensor(bppd[0:32, TB * ch:TB * (ch + 1)],
                                                   bstd[ch % 2][:], ps_bcd[:], ALU.mult))
                    v.memset(bppd[32:33, :], -Q_ZERO)

                @block.tensor
                def _(t):
                    for ch in range(NCHB):
                        t.wait_ge(s_rsc6.h, ch + 1)
                        if ch >= 1:
                            t.wait_ge(s_btt6.h, ch)
                        s_bmm6.tag(t.matmul(ps_bcd[:], one1x32d[:], rscb[ch % 2][:],
                                            start=True, stop=True))

        ps_d = [st.enter_context(nc.psum_tensor(f"psd{i}", [128, TB], F32)) for i in range(2)]
        ps_t = [st.enter_context(nc.psum_tensor(f"pst{i}", [128, TB], BF16)) for i in range(2)]
        ps_ud = st.enter_context(nc.psum_tensor("psud", [33, TB], F32))

        NTOT = NTB * FC
        with nc.Block() as block:
            @block.gpsimd
            def _(g):
                g.wait_ge(s_pc.h, 16 * 86)
                n = 0
                for tc in range(NTB):
                    for oc in range(FC):
                        n += 1
                        if n > 2:
                            g.wait_ge(s_d.h, n - 2)
                        s_dsl.tag(n - 1, g.dma_start(dsb[n % 2][:], slab_ap(dbf_s, 128 * oc, 128, IC)))

            @block.sync
            def _(sy):
                n = 0
                for tc in range(NTB):
                    if tc >= 1:
                        sy.wait_ge(s_d.h, FC * tc)
                    for piece in range(4):
                        ic0 = piece * 22
                        icn = min(22, IC - ic0)
                        if icn <= 0:
                            continue
                        ap_in = bass.AP(h_s, 128 * ic0 * S_LEN + TB * tc,
                                        [[S_LEN, 128], [128 * S_LEN, icn], [1, TB]])
                        s_hld[piece].tag(tc, sy.dma_start(sb3(hres, TB * ic0, (TB, icn), (1, TB)), ap_in))
                    for oc in range(FC):
                        n += 1
                        if n > 2:
                            sy.wait_ge(s_mm6.h, n - 2)
                        s_sld.tag(n - 1, sy.dma_start(st_sb[n % 2][:],
                                                      sT_s.ap()[128 * oc:128 * (oc + 1), TB * tc:TB * (tc + 1)]))
                        if n > 2:
                            sy.wait_ge(s_o6.h, n - 2)
                        s_xld.tag(n - 1, sy.dma_start(x_sb6[n % 2][:],
                                                      bass.AP(x_in, TB * tc * HID + 128 * oc,
                                                              [[HID, 128], [128 * HID, 4], [1, 128]])))
                        if n >= 2:
                            sy.wait_ge(s_o6.h, n - 1)
                            tcp, ocp = divmod(n - 2, FC)
                            s_ost.tag(n - 2, sy.dma_start(bass.AP(out, TB * tcp * HID + 128 * ocp,
                                                                  [[HID, 128], [128 * HID, 4], [1, 128]]),
                                                          o_sb[(n - 1) % 2][:]))
                sy.wait_ge(s_o6.h, NTOT)
                tcp, ocp = divmod(NTOT - 1, FC)
                s_ost.tag(NTOT - 1, sy.dma_start(bass.AP(out, TB * tcp * HID + 128 * ocp,
                                                         [[HID, 128], [128 * HID, 4], [1, 128]]),
                                                 o_sb[NTOT % 2][:]))
                s_ost.wait_all(sy)

            @block.tensor
            def _(t):
                n = 0
                for tc in range(NTB):
                    if tc >= 1:
                        t.wait_ge(s_udev.h, tc)
                    for ic in range(IC):
                        if ic % 22 == 0:
                            s_hld[ic // 22].wait(t, tc)
                        ins = t.matmul(ps_ud[:], aaugd[:, 33 * ic:33 * (ic + 1)],
                                       hres[:, TB * ic:TB * (ic + 1)],
                                       start=(ic == 0), stop=(ic == IC - 1))
                    s_ud.tag(ins)
                    for oc in range(FC):
                        n += 1
                        s_dsl.wait(t, n - 1)
                        if n > 2:
                            t.wait_ge(s_g.h, n - 2)
                        if oc == 0:
                            t.wait_ge(s_udev.h, tc + 1)
                        for ic in range(IC):
                            t.matmul(ps_d[n % 2][:], dsb[n % 2][:, 128 * ic:128 * (ic + 1)],
                                     hres[:, TB * ic:TB * (ic + 1)],
                                     start=(ic == 0), stop=False)
                        s_d.tag(t.matmul(ps_d[n % 2][:], bppd[:, 128 * oc:128 * (oc + 1)],
                                         ud_sb[:], start=False, stop=True))
                        if n >= 2:
                            t.wait_ge(s_mm6.h, n - 1)
                            if n > 3:
                                t.wait_ge(s_o6.h, n - 3)
                            for sblk in range(4):
                                ins = t.transpose(ps_t[(n - 1) % 2][:, 128 * sblk:128 * (sblk + 1)],
                                                  m_sb[(n - 1) % 2][:, 128 * sblk:128 * (sblk + 1)],
                                                  ident[:])
                            s_t6.tag(ins)
                t.wait_ge(s_mm6.h, NTOT)
                t.wait_ge(s_o6.h, NTOT - 2)
                for sblk in range(4):
                    ins = t.transpose(ps_t[NTOT % 2][:, 128 * sblk:128 * (sblk + 1)],
                                      m_sb[NTOT % 2][:, 128 * sblk:128 * (sblk + 1)], ident[:])
                s_t6.tag(ins)

            @block.scalar
            def _(sc):
                n = 0
                for tc in range(NTB):
                    sc.wait_ge(s_ud.h, tc + 1)
                    s_udev.tag(sc.copy(ud_sb[:], ps_ud[:]))
                    for oc in range(FC):
                        n += 1
                        sc.wait_ge(s_d.h, n)
                        if n > 2:
                            sc.wait_ge(s_mm6.h, n - 2)
                        s_g.tag(sc.activation(g_sb[n % 2][:], ps_d[n % 2][:], AF.Silu,
                                              scale=spcd[:, oc:oc + 1]))

            @block.vector
            def _(v):
                n = 0
                for tc in range(NTB):
                    for oc in range(FC):
                        n += 1
                        v.wait_ge(s_g.h, n)
                        s_sld.wait(v, n - 1)
                        if n > 2:
                            v.wait_ge(s_t6.h, n - 2)
                        s_mm6.tag(v.tensor_tensor(m_sb[n % 2][:], g_sb[n % 2][:], st_sb[n % 2][:], ALU.mult))
                        if n >= 2:
                            v.wait_ge(s_t6.h, n - 1)
                            s_xld.wait(v, n - 2)
                            if n > 3:
                                s_ost.wait(v, n - 4)
                            s_o6.tag(v.tensor_tensor(o_sb[(n - 1) % 2][:], ps_t[(n - 1) % 2][:],
                                                     x_sb6[(n - 1) % 2][:], ALU.add))
                v.wait_ge(s_t6.h, NTOT)
                s_xld.wait(v, NTOT - 1)
                s_ost.wait(v, NTOT - 3)
                s_o6.tag(v.tensor_tensor(o_sb[NTOT % 2][:], ps_t[NTOT % 2][:], x_sb6[NTOT % 2][:], ALU.add))


def stage_inputs(inputs):
    a = {k: np.asarray(v) for k, v in inputs.items()}
    common = {
        "wqT": np.ascontiguousarray(a["L0_attn_Wq"].T),
        "wkT": np.ascontiguousarray(a["L0_attn_Wk"].T),
        "wvT": np.ascontiguousarray(a["L0_attn_Wv"].T),
        "woT": np.ascontiguousarray(a["L0_attn_Wo"].T),
        "qgT": np.ascontiguousarray(a["L0_ffn_gate_q"].T),
        "quT": np.ascontiguousarray(a["L0_ffn_up_q"].T),
        "qdT": np.ascontiguousarray(a["L0_ffn_down_q"].T),
        "gsc_pc": np.ascontiguousarray(a["L0_ffn_gate_scale"].reshape(IC, 128).T),
        "usc_pc": np.ascontiguousarray(a["L0_ffn_up_scale"].reshape(IC, 128).T),
        "dsc_pc": np.ascontiguousarray(a["L0_ffn_down_scale"].reshape(FC, 128).T),
        "gsc_row": np.ascontiguousarray(a["L0_ffn_gate_scale"].reshape(1, IMED)),
        "usc_row": np.ascontiguousarray(a["L0_ffn_up_scale"].reshape(1, IMED)),
        "dsc_row": np.ascontiguousarray(a["L0_ffn_down_scale"].reshape(1, HID)),
        "gA": np.ascontiguousarray(a["L0_ffn_gate_A"]),
        "uA": np.ascontiguousarray(a["L0_ffn_up_A"]),
        "dA": np.ascontiguousarray(a["L0_ffn_down_A"]),
        "gB": np.ascontiguousarray(a["L0_ffn_gate_B"]),
        "uB": np.ascontiguousarray(a["L0_ffn_up_B"]),
        "dB": np.ascontiguousarray(a["L0_ffn_down_B"]),
    }
    x = a["L0_attn_hidden_states"]
    return [dict(common, x=np.ascontiguousarray(x[c])) for c in range(8)]


def kernel(**inputs):
    in_maps = stage_inputs(inputs)
    nc = build_program()
    res = run_bass_kernel_spmd(nc, in_maps, core_ids=list(range(8)))
    return np.stack([r["out"] for r in res.results]).astype(np.float32)
